# version 03
# baseline (speedup 1.0000x reference)
import math
import numpy as np
import ml_dtypes
import concourse.bass as bass
import concourse.mybir as mybir
from concourse.bass_utils import run_bass_kernel_spmd

F32 = mybir.dt.float32
BF16 = mybir.dt.bfloat16
ALU = mybir.AluOpType
AF = mybir.ActivationFunctionType
AX = mybir.AxisListType

D = 2048
S = 4096
NCORE = 8
NHC = 4
DK = 128
HF = 8192
GU = 2 * HF
OWN = 1024
QTS = 256
INVSQ = float(1.0 / math.sqrt(128.0))
LN_EPS = 1e-5
RMS_EPS = float(np.float32(1.1920929e-7))
MAG = 192.0
KCMAX = 34
AGR = 3072

ENGS = ("sync", "tensor", "vector", "scalar", "gpsimd")
OWN_SEM = {"tensor": "t", "vector": "v", "scalar": "a", "gpsimd": "g"}

PHASES = ("P", "Q", "A", "F", "C")
DBG_DUMPS = ()
P_TILES = 32
A_HEADS = NHC if 'NHC' in dir() else 4


class Planner:
    def __init__(self):
        self.ops = {e: [] for e in ENGS}
        self.cnt = {}
        self.maxwait = {}
        self.uninc = {e: 0 for e in ENGS}

    def sem(self, *names):
        for n in names:
            self.cnt[n] = 0
            self.maxwait[n] = 0

    def op(self, eng, fn, waits=(), inc=None, amt=1, real=True, flush=False):
        if flush:
            own = OWN_SEM[eng]
            waits = tuple(waits) + ((own, self.cnt[own]),)
            inc = inc or own
        w = tuple((s, v) for (s, v) in waits if v > 0)
        for s, v in w:
            assert s in self.cnt, s
            self.maxwait[s] = max(self.maxwait[s], v)
        if inc is not None:
            self.cnt[inc] += amt
        if eng in OWN_SEM:
            if real:
                self.uninc[eng] += 1
                if inc == OWN_SEM[eng]:
                    self.uninc[eng] = 0

        def thunk(e, SM, fn=fn, w=w, inc=inc, amt=amt):
            for s, v in w:
                e.wait_ge(SM[s], v)
            r = fn(e)
            if inc is not None:
                r.then_inc(SM[inc], amt)
        self.ops[eng].append(thunk)
        return self.cnt[inc] if inc is not None else None

    def wait(self, eng, *pairs):
        w = tuple((s, v) for (s, v) in pairs if v > 0)
        if not w:
            return
        for s, v in w:
            self.maxwait[s] = max(self.maxwait[s], v)

        def thunk(e, SM, w=w):
            for s, v in w:
                e.wait_ge(SM[s], v)
        self.ops[eng].append(thunk)

    def dma(self, queue, out, in_, waits=(), accum=False):
        sem = {"sync": "sd", "gpsimd": "sg", "vector": "sv", "scalar": "sa"}[queue]
        if accum:
            return self.op(queue, lambda e: e.dma_start(out=out, in_=in_, accum_op=ALU.add),
                           waits=waits, inc=sem, amt=16, real=False)
        return self.op(queue, lambda e: e.dma_start(out=out, in_=in_),
                       waits=waits, inc=sem, amt=16, real=False)

    def barrier(self):
        for e in ("tensor", "vector", "scalar", "gpsimd"):
            assert self.uninc[e] == 0, f"barrier: engine {e} has {self.uninc[e]} un-inc'd ops"
        drains = (("sd", self.cnt["sd"]), ("sg", self.cnt["sg"]),
                  ("sv", self.cnt["sv"]), ("sa", self.cnt["sa"]))
        base = self.cnt["bar"]
        for e in ENGS:
            own = (((OWN_SEM[e], self.cnt[OWN_SEM[e]]),) if e in OWN_SEM else ())
            self.op(e, lambda en: en.nop(), waits=drains + own, inc="bar", real=False)
        for e in ENGS:
            self.wait(e, ("bar", base + len(ENGS)))

    def check(self):
        for s, v in self.maxwait.items():
            assert v <= self.cnt[s], f"sem {s}: wait {v} > final {self.cnt[s]}"


class Arena:
    NCOLS = 50176

    def __init__(self, handle):
        self.ptr = 0
        self.named = {}
        self.h = handle

    def alloc(self, name, cols):
        assert self.ptr + cols <= self.NCOLS, f"{name}: OOM {self.ptr}+{cols}"
        self.named[name] = (self.ptr, cols)
        self.ptr += cols

    def alias(self, name, base_name, rel_off, cols):
        o, n = self.named[base_name]
        assert rel_off + cols <= n, f"alias {name} exceeds {base_name}"
        self.named[name] = (o + rel_off, cols)

    def mark(self):
        return self.ptr

    def reset(self, p):
        self.ptr = p

    def f32(self, name):
        o, n = self.named[name]
        return self.h[:, o:o + n]

    def bf16(self, name):
        o, n = self.named[name]
        return self.h[:, o:o + n].bitcast(BF16)


def build_core_program():
    nc = bass.Bass("TRN2", target_bir_lowering=False, detect_race_conditions=False)
    pl = Planner()
    pl.sem("sd", "sg", "sv", "sa", "t", "v", "a", "g", "cc", "bar")

    x_seq = nc.dram_tensor("x_seq", [S, D], F32, kind="ExternalInput").ap()
    wqkvT = nc.dram_tensor("wqkvT", [D, 3 * NHC * DK], F32, kind="ExternalInput").ap()
    woutT = nc.dram_tensor("woutT", [NHC * DK, D], F32, kind="ExternalInput").ap()
    wgu_sh = nc.dram_tensor("wgu_sh", [GU // NCORE, D], F32, kind="ExternalInput").ap()
    wp_sh = nc.dram_tensor("wp_sh", [D // NCORE, HF], F32, kind="ExternalInput").ap()
    ident_in = nc.dram_tensor("ident_in", [128, 128], F32, kind="ExternalInput").ap()
    masks_in = nc.dram_tensor("masks_in", [128, 4 * 512], BF16, kind="ExternalInput").ap()
    x_own = nc.dram_tensor("x_own", [OWN, D], F32, kind="ExternalInput").ap()
    out_c = nc.dram_tensor("out_c", [OWN, D], F32, kind="ExternalOutput").ap()

    xsp_h = nc.dram_tensor("xsp_h", [D, S], BF16, kind="Internal").ap()
    xsp_l = nc.dram_tensor("xsp_l", [D, S], BF16, kind="Internal").ap()
    rs_in = nc.dram_tensor("rs_in", [S, D], F32, kind="Internal").ap()
    rs_out = nc.dram_tensor("rs_out", [OWN, D], F32, kind="Internal").ap()
    ag_in = nc.dram_tensor("ag_in", [AGR, D], BF16, kind="Internal").ap()
    ag_out = nc.dram_tensor("ag_out", [NCORE * AGR, D], BF16, kind="Internal",
                            addr_space="Shared").ap()

    dbg = {}
    for name in DBG_DUMPS:
        shapes = {"xsp": [D, S], "ag": [NCORE * AGR, D], "kvq": [6 * 128, S],
                  "rsin": [S, D], "rsout": [OWN, D], "gbuf": [OWN, D],
                  "xnf": [128, D], "tiny": [128, 64],
                  "ssb": [128, S], "psb": [128, S], "pt": [128, KCMAX * 512],
                  "oh": [128, 512], "rstg": [128, D],
                  "vtf": [128, 256], "vpre": [2 * 128, S], "ps5": [128, 512],
                  "w32d": [128, 2048], "whd": [128, 2048]}
        dt = {"xsp": BF16, "ag": BF16, "kvq": BF16, "rsin": F32, "rsout": F32,
              "gbuf": F32, "xnf": F32, "tiny": F32,
              "ssb": F32, "psb": F32, "pt": BF16, "oh": BF16, "rstg": F32,
              "vtf": F32, "vpre": BF16, "ps5": F32, "w32d": F32, "whd": BF16}[name]
        dbg[name] = nc.dram_tensor("dbg_" + name, shapes[name], dt, kind="ExternalOutput").ap()

    arena_h = nc.alloc_sbuf_tensor("arena", [128, Arena.NCOLS], F32)
    ar = Arena(arena_h.ap())
    PS = [nc.alloc_psum_tensor(f"ps{i}", [128, 512], F32).ap() for i in range(8)]

    ar.alloc("ident", 128)
    ar.alloc("masks", 1024)
    ar.alloc("tiny", 64)
    T_SUMX, T_SUMSQ, T_MEAN, T_VAR, T_INV, T_NMI, T_TMP = 0, 1, 2, 3, 4, 5, 6
    T_NGM, T_NGC, T_RSR, T_RSC, T_RRC = 8, 10, 12, 14, 16
    T_SSQ, T_SQC, T_FIN = 24, 32, 40

    def tiny(slot):
        o, _ = ar.named["tiny"]
        return ar.h[:, o + slot:o + slot + 1]

    g_base = ar.mark()
    ident = ar.f32("ident")

    pl.dma("sync", ident, ident_in)
    sd_const = pl.dma("sync", ar.bf16("masks"), masks_in)

    bank_free = {i: [] for i in range(8)}

    def mm_bank_wait(bank):
        pl.wait("tensor", *bank_free[bank])
        bank_free[bank] = []

    def emit_rsqrt(x_slot, out_slot, s_slot):
        pl.op("vector", lambda e: e.tensor_scalar(
            out=tiny(s_slot), in0=tiny(x_slot), scalar1=0.5, scalar2=0.5,
            op0=ALU.mult, op1=ALU.add), flush=True)
        for _ in range(7):
            pl.op("vector", lambda e: e.reciprocal(out=tiny(out_slot), in_=tiny(s_slot)), flush=True)
            pl.op("vector", lambda e: e.tensor_tensor(
                out=tiny(out_slot), in0=tiny(x_slot), in1=tiny(out_slot), op=ALU.mult), flush=True)
            pl.op("vector", lambda e: e.tensor_tensor(
                out=tiny(s_slot), in0=tiny(s_slot), in1=tiny(out_slot), op=ALU.add), flush=True)
            pl.op("vector", lambda e: e.tensor_scalar_mul(
                out=tiny(s_slot), in0=tiny(s_slot), scalar1=0.5), flush=True)
        return pl.op("vector", lambda e: e.reciprocal(
            out=tiny(out_slot), in_=tiny(s_slot)), flush=True)

    def emit_ln(xb, xnf, sqscr, sd_x, t_guard):
        v_guard = pl.cnt["v"]
        pl.op("vector", lambda e: e.tensor_reduce(
            out=tiny(T_SUMX), in_=xb, axis=AX.X, op=ALU.add), waits=[("sd", sd_x)],
            flush=True)
        pl.op("scalar", lambda e: e.preload_act_table(AF.Square))
        a_sq = pl.op("scalar", lambda e: e.activation(
            out=sqscr, in_=xb, func=AF.Square, accum_out=tiny(T_SUMSQ)),
            waits=[("sd", sd_x), ("v", v_guard)], inc="a")
        pl.op("vector", lambda e: e.tensor_scalar_mul(out=tiny(T_MEAN), in0=tiny(T_SUMX), scalar1=1.0 / D),
              flush=True)
        pl.op("vector", lambda e: e.tensor_scalar_mul(out=tiny(T_VAR), in0=tiny(T_SUMSQ), scalar1=1.0 / D),
              waits=[("a", a_sq)], flush=True)
        pl.op("vector", lambda e: e.tensor_tensor(out=tiny(T_TMP), in0=tiny(T_MEAN), in1=tiny(T_MEAN), op=ALU.mult),
              flush=True)
        pl.op("vector", lambda e: e.tensor_tensor(out=tiny(T_VAR), in0=tiny(T_VAR), in1=tiny(T_TMP), op=ALU.subtract),
              flush=True)
        pl.op("vector", lambda e: e.tensor_scalar_add(out=tiny(T_VAR), in0=tiny(T_VAR), scalar1=LN_EPS),
              flush=True)
        emit_rsqrt(T_VAR, T_INV, T_TMP)
        pl.op("vector", lambda e: e.tensor_scalar_mul(out=tiny(T_NMI), in0=tiny(T_MEAN), scalar1=-1.0),
              flush=True)
        v_xnf = pl.op("vector", lambda e: e.tensor_scalar(
            out=xnf, in0=xb, scalar1=tiny(T_NMI), scalar2=tiny(T_INV),
            op0=ALU.add, op1=ALU.mult),
            waits=[("t", t_guard)], inc="v", flush=True)
        return a_sq, v_xnf

    def emit_ln_tr_phase(x_src, n_tiles, writer, xnf_name, sq_name,
                         xb0_name, xb1_name, hi_stg, lo_stg):
        done_xb = [(0, 0), (0, 0)]
        wr_guard = (None, 0)
        for i in range(n_tiles):
            bsel = i % 2
            xb = ar.f32(xb0_name) if bsel == 0 else ar.f32(xb1_name)
            sd_x = pl.dma("sync", xb, x_src[i * 128:(i + 1) * 128, :],
                          waits=[("a", done_xb[bsel][0]), ("v", done_xb[bsel][1])])
            a_sq, v_xnf = emit_ln(xb, ar.f32(xnf_name), ar.f32(sq_name), sd_x, pl.cnt["t"])
            done_xb[bsel] = (a_sq, v_xnf)
            pb = 0 if bsel == 0 else 4
            bw = []
            for b in range(4):
                bw += bank_free[pb + b]
                bank_free[pb + b] = []
            pl.wait("tensor", ("v", v_xnf), ("sd", sd_const), *bw)
            t_tr = None
            for dc in range(16):
                bank, col = pb + dc // 4, (dc % 4) * 128
                t_tr = pl.op("tensor", lambda e, dc=dc, bank=bank, col=col: e.transpose(
                    PS[bank][:, col:col + 128],
                    ar.f32(xnf_name)[:, dc * 128:(dc + 1) * 128], ident),
                    inc="t" if dc == 15 else None)
            a_hi = v_lo = 0
            for b in range(4):
                ew = [("t", t_tr)]
                if b == 0 and wr_guard[0] is not None:
                    ew.append(wr_guard)
                a_hi = pl.op("scalar", lambda e, pb=pb, b=b: e.activation(
                    out=hi_stg[:, b * 512:(b + 1) * 512], in_=PS[pb + b][:, :], func=AF.Copy),
                    waits=ew, inc="a")
                v_lo = pl.op("vector", lambda e, pb=pb, b=b: e.tensor_tensor(
                    out=lo_stg[:, b * 512:(b + 1) * 512], in0=PS[pb + b][:, :],
                    in1=hi_stg[:, b * 512:(b + 1) * 512], op=ALU.subtract),
                    waits=[("a", a_hi)], inc="v")
                bank_free[pb + b] = [("v", v_lo)]
            wr_guard = writer(i, a_hi, v_lo)

    if "P" in PHASES:
        ar.reset(g_base)
        ar.alloc("xbuf0", D)
        ar.alloc("xbuf1", D)
        ar.alloc("sqscr", D)
        ar.alloc("xnf", D)
        ar.alloc("trsh", 1024)
        ar.alloc("trsl", 1024)

        def p_writer(i, a_hi, v_lo):
            src_h = ar.bf16("trsh").rearrange("p (dc t) -> p dc t", t=128)
            dst_h = xsp_h.rearrange("(dc p) (i t) -> p dc i t", p=128, t=128)[:, :, i, :]
            pl.dma("sync", dst_h, src_h, waits=[("a", a_hi), ("v", v_lo)])
            src_l = ar.bf16("trsl").rearrange("p (dc t) -> p dc t", t=128)
            dst_l = xsp_l.rearrange("(dc p) (i t) -> p dc i t", p=128, t=128)[:, :, i, :]
            sd_w = pl.dma("sync", dst_l, src_l)
            return ("sd", sd_w)

        emit_ln_tr_phase(x_seq, P_TILES, p_writer, "xnf", "sqscr",
                         "xbuf0", "xbuf1", ar.bf16("trsh"), ar.bf16("trsl"))
        pl.barrier()
        if "xnf" in dbg:
            pl.dma("gpsimd", dbg["xnf"], ar.f32("xnf"))
        if "tiny" in dbg:
            pl.dma("gpsimd", dbg["tiny"], ar.f32("tiny"))
        if "xsp" in dbg:
            pl.dma("gpsimd", dbg["xsp"], xsp_h)
        if dbg:
            pl.barrier()

    if "Q" in PHASES:
        ar.reset(g_base)
        ar.alloc("qwf", HF)
        ar.alloc("qwb", HF // 2)
        ar.alloc("qratio", HF // 2)
        ar.alloc("qsc", 128)
        ar.alloc("qscb", 64)
        ar.alloc("qrec", 128)
        ar.alloc("qst", 512)
        qst_free = 0
        tile_free = 0

        def quant_tile(src_ap, w, ngrp, dst_writer):
            nonlocal qst_free, tile_free
            sg_b = pl.dma("gpsimd", ar.bf16("qwb")[:, :w], src_ap, waits=[("t", tile_free)])
            qwb_g = ar.bf16("qwb")[:, :w].rearrange("p (g k) -> p g k", k=64)
            pl.op("vector", lambda e: e.tensor_reduce(
                out=ar.f32("qsc")[:, :ngrp], in_=qwb_g, axis=AX.X, op=ALU.add,
                apply_absolute_value=True), waits=[("sg", sg_b), ("t", tile_free)],
                flush=True)
            pl.op("vector", lambda e: e.tensor_scalar(
                out=ar.bf16("qscb")[:, :ngrp], in0=ar.f32("qsc")[:, :ngrp],
                scalar1=1.0 / 64.0, scalar2=1e-8, op0=ALU.mult, op1=ALU.max),
                flush=True)
            pl.op("vector", lambda e: e.tensor_copy(
                out=ar.f32("qrec")[:, :ngrp], in_=ar.bf16("qscb")[:, :ngrp]),
                flush=True)
            pl.op("vector", lambda e: e.reciprocal(
                out=ar.f32("qrec")[:, :ngrp], in_=ar.f32("qrec")[:, :ngrp]),
                flush=True)
            recb = ar.f32("qrec")[:, :ngrp].unsqueeze(2).broadcast_to((128, ngrp, 64))
            sclb = ar.bf16("qscb")[:, :ngrp].unsqueeze(2).broadcast_to((128, ngrp, 64))
            ratio_b = ar.bf16("qratio")[:, :w]
            ratio_g = ratio_b.rearrange("p (g k) -> p g k", k=64)
            pl.op("vector", lambda e: e.tensor_tensor(out=ratio_g, in0=qwb_g, in1=recb, op=ALU.mult),
                flush=True)
            pl.op("vector", lambda e: e.tensor_scalar_add(out=ratio_b, in0=ratio_b, scalar1=MAG))
            pl.op("vector", lambda e: e.tensor_scalar(
                out=ratio_b, in0=ratio_b, scalar1=MAG, scalar2=1.0,
                op0=ALU.subtract, op1=ALU.min))
            v3 = pl.op("vector", lambda e: e.scalar_tensor_tensor(
                out=ar.f32("qwf")[:, :w].rearrange("p (g k) -> p g k", k=64),
                in0=ratio_g, scalar=-1.0, in1=sclb,
                op0=ALU.max, op1=ALU.mult), inc="v")
            nblk = w // 128
            for b0 in range(0, nblk, 8):
                nb = min(8, nblk - b0)
                pl.wait("tensor", ("v", v3), ("sd", qst_free),
                        *(bank_free[0] + bank_free[1]))
                bank_free[0] = []
                bank_free[1] = []
                t_q = None
                for j in range(nb):
                    bank, col = j // 4, (j % 4) * 128
                    t_q = pl.op("tensor", lambda e, j=j, b0=b0, bank=bank, col=col: e.transpose(
                        PS[bank][:, col:col + 128],
                        ar.f32("qwf")[:, (b0 + j) * 128:(b0 + j + 1) * 128], ident),
                        inc="t" if j == nb - 1 else None)
                a_e = pl.op("scalar", lambda e: e.activation(
                    out=ar.bf16("qst")[:, 0:512], in_=PS[0][:, :], func=AF.Copy),
                    waits=[("t", t_q)], inc="a")
                if nb > 4:
                    a_e = pl.op("scalar", lambda e: e.activation(
                        out=ar.bf16("qst")[:, 512:1024], in_=PS[1][:, :], func=AF.Copy),
                        inc="a")
                bank_free[0] = [("a", a_e)]
                bank_free[1] = [("a", a_e)]
                qst_free = dst_writer(b0, nb, a_e)
            tile_free = pl.cnt["t"]

        for ht in range(16):
            def wr_gu(b0, nb, a_e, ht=ht):
                src = ar.bf16("qst")[:, :nb * 128].rearrange("p (j c) -> p j c", c=128)
                dst = ag_in.rearrange("(dc p) (hc c) -> p dc hc c", p=128, c=128)[:, b0:b0 + nb, ht, :]
                return pl.dma("sync", dst, src, waits=[("a", a_e)])
            quant_tile(wgu_sh[ht * 128:(ht + 1) * 128, :], D, 32, wr_gu)
        pT = ag_in[2048:AGR, :].rearrange("r c -> (r c)").rearrange("(h e) -> h e", e=256)
        for et in range(2):
            def wr_p(b0, nb, a_e, et=et):
                src = ar.bf16("qst")[:, :nb * 128].rearrange("p (j c) -> p j c", c=128)
                dst = pT.rearrange("(j p) e -> p j e", p=128)[:, b0:b0 + nb, et * 128:(et + 1) * 128]
                return pl.dma("sync", dst, src, waits=[("a", a_e)])
            quant_tile(wp_sh[et * 128:(et + 1) * 128, :], HF, 128, wr_p)

        pl.barrier()
        pl.op("gpsimd", lambda e: e.collective_compute(
            "AllGather", ALU.bypass, replica_groups=[list(range(NCORE))],
            ins=[ag_in.opt()], outs=[ag_out.opt()]), inc="cc", real=False)
        if "ag" in dbg:
            pl.wait("gpsimd", ("cc", 1))
            pl.dma("gpsimd", dbg["ag"], ag_out)
            pl.barrier()

    if "A" in PHASES:
        ar.reset(g_base)
        ar.alloc("KTh", D)
        ar.alloc("KTl", D)
        ar.alloc("QTh", D)
        ar.alloc("QTl", D)
        ar.alloc("Vth", D)
        ar.alloc("Vtl", D)
        ar.alloc("WOh", D // 2)
        ar.alloc("WOl", D // 2)
        ar.alloc("rstg", D)
        ar.alloc("vtf", 256)
        a_scr = ar.mark()
        ar.alloc("wq32", 16 * 384)
        ar.alloc("wqh", 16 * 384 // 2)
        ar.alloc("wql", 16 * 384 // 2)
        ar.alloc("xsl_h0", 2048)
        ar.alloc("xsl_l0", 2048)
        ar.alloc("xsl_h1", 2048)
        ar.alloc("xsl_l1", 2048)
        a2_base = a_scr

        def phase_a1(h):
            wo_src = woutT[h * 128:(h + 1) * 128, :]
            sd_wo = pl.dma("sync", ar.f32("wq32")[:, :D], wo_src)
            sg_wo = pl.dma("gpsimd", ar.bf16("WOh"), wo_src)
            v_wo = pl.op("vector", lambda e: e.tensor_tensor(
                out=ar.bf16("WOl"), in0=ar.f32("wq32")[:, :D], in1=ar.bf16("WOh"), op=ALU.subtract),
                waits=[("sd", sd_wo), ("sg", sg_wo)], inc="v")
            w32 = ar.f32("wq32").rearrange("p (j dc c) -> p j dc c", j=3, c=128)
            wh = ar.bf16("wqh").rearrange("p (j dc c) -> p j dc c", j=3, c=128)
            wl = ar.bf16("wql").rearrange("p (j dc c) -> p j dc c", j=3, c=128)
            cols = [h * 128, NHC * 128 + h * 128, 2 * NHC * 128 + h * 128]
            sd_w = 0
            for j, c in enumerate(cols):
                src = wqkvT[:, c:c + 128].rearrange("(dc p) e -> p dc e", p=128)
                sd_w = pl.dma("sync", w32[:, j], src,
                              waits=[("v", v_wo)] if j == 0 else ())
            whf, wlf = ar.bf16("wqh"), ar.bf16("wql")
            w32f = ar.f32("wq32")
            pl.op("scalar", lambda e: e.activation(
                out=whf[:, 0:2048], in_=w32f[:, 0:2048], func=AF.Copy, scale=INVSQ),
                waits=[("sd", sd_w)])
            pl.op("scalar", lambda e: e.activation(
                out=whf[:, 2048:4096], in_=w32f[:, 2048:4096], func=AF.Copy))
            a_wh = pl.op("scalar", lambda e: e.activation(
                out=whf[:, 4096:6144], in_=w32f[:, 4096:6144], func=AF.Copy), inc="a")
            pl.op("vector", lambda e: e.scalar_tensor_tensor(
                out=wlf[:, 0:2048], in0=w32f[:, 0:2048], scalar=INVSQ,
                in1=whf[:, 0:2048], op0=ALU.mult, op1=ALU.subtract),
                waits=[("a", a_wh)])
            v_wl = pl.op("vector", lambda e: e.tensor_tensor(
                out=wlf[:, 2048:6144], in0=w32f[:, 2048:6144], in1=whf[:, 2048:6144],
                op=ALU.subtract), inc="v")
            sd_prev = sd_w
            for sp in range(16):
                bsel = sp % 2
                xh = ar.bf16("xsl_h0") if bsel == 0 else ar.bf16("xsl_h1")
                xl = ar.bf16("xsl_l0") if bsel == 0 else ar.bf16("xsl_l1")
                dwaits = [("sd", sd_prev)] + ([("t", pl.cnt["t"])] if sp >= 2 else [])
                xh3 = xh.rearrange("p (dc s) -> p dc s", s=256)
                xl3 = xl.rearrange("p (dc s) -> p dc s", s=256)
                pl.dma("sync", xh3, xsp_h.rearrange("(dc p) s -> p dc s", p=128)[:, :, sp * 256:(sp + 1) * 256],
                       waits=dwaits)
                sd_sl = pl.dma("sync", xl3, xsp_l.rearrange("(dc p) s -> p dc s", p=128)[:, :, sp * 256:(sp + 1) * 256])
                sd_prev = sd_sl
                pb = 0 if bsel == 0 else 3
                pl.wait("tensor", ("sd", sd_sl), ("v", v_wl))
                for bank in (pb, pb + 1, pb + 2):
                    mm_bank_wait(bank)
                t_p = None
                for gi, (proj, bank) in enumerate(((2, pb + 0), (1, pb + 1), (0, pb + 2))):
                    first = True
                    for dc in range(16):
                        whc = wh[:, proj, dc, :]
                        wlc = wl[:, proj, dc, :]
                        xhc = xh3[:, dc, :]
                        xlc = xl3[:, dc, :]
                        pairs = ((whc, xhc), (wlc, xhc), (whc, xlc))
                        for pi, (lt, rt) in enumerate(pairs):
                            last_all = (gi == 2) and (dc == 15) and (pi == 2)
                            t_p = pl.op("tensor", lambda e, lt=lt, rt=rt, bank=bank,
                                        first=first, last=(dc == 15 and pi == 2): e.matmul(
                                PS[bank][:, :256], lt, rt, start=first, stop=last),
                                inc="t" if last_all else None)
                            first = False
                co = sp * 256
                v_vf = pl.op("vector", lambda e, pb=pb: e.tensor_copy(
                    out=ar.f32("vtf"), in_=PS[pb][:, :256]),
                    waits=[("t", t_p)], inc="v")
                bank_free[pb] = [("v", v_vf)]
                a_k = pl.op("scalar", lambda e, pb=pb, co=co: e.activation(
                    out=ar.bf16("KTh")[:, co:co + 256], in_=PS[pb + 1][:, :256], func=AF.Copy),
                    waits=[("t", t_p)], inc="a")
                v_k = pl.op("vector", lambda e, pb=pb, co=co: e.tensor_tensor(
                    out=ar.bf16("KTl")[:, co:co + 256], in0=PS[pb + 1][:, :256],
                    in1=ar.bf16("KTh")[:, co:co + 256], op=ALU.subtract),
                    waits=[("a", a_k)], inc="v")
                bank_free[pb + 1] = [("v", v_k)]
                a_q = pl.op("scalar", lambda e, pb=pb, co=co: e.activation(
                    out=ar.bf16("QTh")[:, co:co + 256], in_=PS[pb + 2][:, :256], func=AF.Copy),
                    inc="a")
                v_q = pl.op("vector", lambda e, pb=pb, co=co: e.tensor_tensor(
                    out=ar.bf16("QTl")[:, co:co + 256], in0=PS[pb + 2][:, :256],
                    in1=ar.bf16("QTh")[:, co:co + 256], op=ALU.subtract),
                    waits=[("a", a_q)], inc="v")
                bank_free[pb + 2] = [("v", v_q)]
                mm_bank_wait(6)
                pl.wait("tensor", ("v", v_vf))
                t_vt = None
                for j in range(2):
                    t_vt = pl.op("tensor", lambda e, j=j: e.transpose(
                        PS[6][:, j * 128:(j + 1) * 128], ar.f32("vtf")[:, j * 128:(j + 1) * 128], ident),
                        inc="t" if j == 1 else None)
                a_vh = pl.op("scalar", lambda e, co=co: e.activation(
                    out=ar.bf16("Vth")[:, co:co + 256], in_=PS[6][:, 0:256], func=AF.Copy),
                    waits=[("t", t_vt)], inc="a")
                v_vl = pl.op("vector", lambda e, co=co: e.tensor_tensor(
                    out=ar.bf16("Vtl")[:, co:co + 256], in0=PS[6][:, 0:256],
                    in1=ar.bf16("Vth")[:, co:co + 256], op=ALU.subtract),
                    waits=[("a", a_vh)], inc="v")
                bank_free[6] = [("v", v_vl)]

        def phase_a2(h, hidx):
            ar.reset(a2_base)
            ar.alloc("S_sb", S)
            ar.alloc("P_sb", S)
            ar.alloc("PT", KCMAX * 256)
            ar.alloc("Oh", 128)
            ar.alloc("Ol", 128)
            sc_ev = {0: [], 1: []}
            pt_ev = {2: [], 3: []}
            rstg_free = 0
            PT_b = ar.bf16("PT").rearrange("p (kc hl sq) -> p kc hl sq", hl=2, sq=256)
            for qi in range(16):
                q0 = QTS * qi
                nch = q0 // 512 + 1
                KC = 2 * qi + 2
                a_guard = pl.cnt["a"]
                pl.op("vector", lambda e: e.memset(tiny(T_NGM + 0), 3e38),
                      waits=[("a", a_guard)], flush=True)
                pl.op("vector", lambda e: e.memset(tiny(T_NGM + 1), 3e38), flush=True)
                pl.op("vector", lambda e: e.memset(tiny(T_RSR + 0), 0.0), flush=True)
                v_init = pl.op("vector", lambda e: e.memset(tiny(T_RSR + 1), 0.0), inc="v",
                               flush=True)
                for s_ in (0, 1):
                    sg_a = pl.cnt["a"]
                    sg_v = pl.cnt["v"]
                    for c in range(nch):
                        bank = c % 2
                        pl.wait("tensor", *sc_ev[bank])
                        sc_ev[bank] = []
                        qh = ar.bf16("QTh")[:, q0 + s_ * 128:q0 + (s_ + 1) * 128]
                        ql = ar.bf16("QTl")[:, q0 + s_ * 128:q0 + (s_ + 1) * 128]
                        kh = ar.bf16("KTh")[:, c * 512:(c + 1) * 512]
                        kl = ar.bf16("KTl")[:, c * 512:(c + 1) * 512]
                        t_sc = None
                        for pi, (lt, rt) in enumerate(((qh, kh), (ql, kh), (qh, kl))):
                            t_sc = pl.op("tensor", lambda e, lt=lt, rt=rt, bank=bank, pi=pi: e.matmul(
                                PS[bank][:, :], lt, rt, start=(pi == 0), stop=(pi == 2)),
                                inc="t" if pi == 2 else None)
                        dst = ar.f32("S_sb")[:, c * 512:(c + 1) * 512]
                        if c == nch - 1:
                            off = (q0 + s_ * 128 - (nch - 1) * 512) // 128
                            mk = ar.bf16("masks")[:, off * 512:(off + 1) * 512]
                            e_ev = pl.op("vector", lambda e, bank=bank, dst=dst, mk=mk: e.tensor_tensor(
                                out=dst, in0=PS[bank][:, :], in1=mk, op=ALU.add),
                                waits=[("t", t_sc), ("v", v_init), ("a", sg_a)], inc="v")
                            sc_ev[bank] = [("v", e_ev)]
                            wv = [("v", e_ev)]
                        else:
                            e_ev = pl.op("scalar", lambda e, bank=bank, dst=dst: e.activation(
                                out=dst, in_=PS[bank][:, :], func=AF.Copy),
                                waits=[("t", t_sc), ("v", sg_v)], inc="a")
                            sc_ev[bank] = [("a", e_ev)]
                            wv = [("a", e_ev)]
                        pl.op("vector", lambda e, dst=dst: e.tensor_reduce(
                            out=tiny(T_NGC), in_=dst, axis=AX.X, op=ALU.max, negate=True),
                            waits=wv + [("v", v_init)], flush=True)
                        v_mx = pl.op("vector", lambda e, s_=s_: e.tensor_tensor(
                            out=tiny(T_NGM + s_), in0=tiny(T_NGM + s_), in1=tiny(T_NGC), op=ALU.min),
                            inc="v", flush=True)
                    v_ac = 0
                    a_x = 0
                    pl.op("scalar", lambda e: e.preload_act_table(AF.Exp))
                    t_pguard = pl.cnt["t"]
                    for c in range(nch):
                        rsl = T_RSC + c % 2
                        a_x = pl.op("scalar", lambda e, c=c, s_=s_, rsl=rsl: e.activation(
                            out=ar.f32("P_sb")[:, c * 512:(c + 1) * 512],
                            in_=ar.f32("S_sb")[:, c * 512:(c + 1) * 512],
                            func=AF.Exp, bias=tiny(T_NGM + s_), accum_out=tiny(rsl)),
                            waits=[("v", v_mx), ("t", t_pguard if c == 0 else 0)]
                            + ([("v", pl.cnt["v"])] if c >= 2 else []), inc="a")
                        v_ac = pl.op("vector", lambda e, s_=s_, rsl=rsl: e.tensor_tensor(
                            out=tiny(T_RSR + s_), in0=tiny(T_RSR + s_), in1=tiny(rsl), op=ALU.add),
                            waits=[("a", a_x)], inc="v", flush=True)
                    v_rr = pl.op("vector", lambda e, s_=s_: e.reciprocal(
                        out=tiny(T_RRC + s_), in_=tiny(T_RSR + s_)),
                        waits=[("v", v_ac), ("a", a_guard)], inc="v", flush=True)
                    for c in range(nch):
                        jmax = min(4, KC - 4 * c)
                        if jmax <= 0:
                            continue
                        ptbank = 2 + c % 2
                        pl.wait("tensor", *pt_ev[ptbank])
                        pt_ev[ptbank] = []
                        pl.wait("tensor", ("a", a_x))
                        t_pt = None
                        for j in range(jmax):
                            t_pt = pl.op("tensor", lambda e, c=c, j=j, ptbank=ptbank: e.transpose(
                                PS[ptbank][:, j * 128:(j + 1) * 128],
                                ar.f32("P_sb")[:, c * 512 + j * 128:c * 512 + (j + 1) * 128], ident),
                                inc="t" if j == jmax - 1 else None)
                        dst_hi = PT_b[:, 4 * c:4 * c + jmax, 0, s_ * 128:(s_ + 1) * 128]
                        dst_lo = PT_b[:, 4 * c:4 * c + jmax, 1, s_ * 128:(s_ + 1) * 128]
                        src3 = PS[ptbank][:, :jmax * 128].rearrange("p (j c) -> p j c", c=128)
                        a_ph = pl.op("scalar", lambda e, src3=src3, dst_hi=dst_hi: e.activation(
                            out=dst_hi, in_=src3, func=AF.Copy),
                            waits=[("t", t_pt)], inc="a")
                        v_pl = pl.op("vector", lambda e, src3=src3, dst_hi=dst_hi, dst_lo=dst_lo: e.tensor_tensor(
                            out=dst_lo, in0=src3, in1=dst_hi, op=ALU.subtract),
                            waits=[("a", a_ph)], inc="v")
                        pt_ev[ptbank] = [("a", a_ph), ("v", v_pl)]
                mm_bank_wait(4)
                pl.wait("tensor", ("v", pl.cnt["v"]), ("a", pl.cnt["a"]))
                t_pv = None
                for kc in range(KC):
                    vh = ar.bf16("Vth")[:, kc * 128:(kc + 1) * 128]
                    vl = ar.bf16("Vtl")[:, kc * 128:(kc + 1) * 128]
                    ph = ar.bf16("PT")[:, kc * 512:kc * 512 + 256]
                    plo = ar.bf16("PT")[:, kc * 512 + 256:kc * 512 + 512]
                    for pi, (lt, rt) in enumerate(((vh, ph), (vl, ph), (vh, plo))):
                        last = (kc == KC - 1 and pi == 2)
                        t_pv = pl.op("tensor", lambda e, lt=lt, rt=rt, kc=kc, pi=pi, last=last: e.matmul(
                            PS[4][:, :256], lt, rt, start=(kc == 0 and pi == 0), stop=last),
                            inc="t" if last else None)
                a_oh = pl.op("scalar", lambda e: e.activation(
                    out=ar.bf16("Oh"), in_=PS[4][:, :256], func=AF.Copy),
                    waits=[("t", t_pv)], inc="a")
                v_ol = pl.op("vector", lambda e: e.tensor_tensor(
                    out=ar.bf16("Ol"), in0=PS[4][:, :256], in1=ar.bf16("Oh"), op=ALU.subtract),
                    waits=[("a", a_oh)], inc="v")
                bank_free[4] = [("v", v_ol)]
                pl.wait("tensor", ("a", a_oh), ("v", v_ol))
                opj_free = {5: [], 0: []}
                last_b0 = None
                for s_ in (0, 1):
                    ev_list = []
                    for eg in range(4):
                        ob = 5 if eg % 2 == 0 else 0
                        pl.wait("tensor", *opj_free[ob])
                        opj_free[ob] = []
                        oh = ar.bf16("Oh")[:, s_ * 128:(s_ + 1) * 128]
                        ol = ar.bf16("Ol")[:, s_ * 128:(s_ + 1) * 128]
                        wh_ = ar.bf16("WOh")[:, eg * 512:(eg + 1) * 512]
                        wl_ = ar.bf16("WOl")[:, eg * 512:(eg + 1) * 512]
                        t_op = None
                        for pi, (lt, rt) in enumerate(((oh, wh_), (ol, wh_), (oh, wl_))):
                            t_op = pl.op("tensor", lambda e, lt=lt, rt=rt, pi=pi, ob=ob: e.matmul(
                                PS[ob][:, :], lt, rt, start=(pi == 0), stop=(pi == 2)),
                                inc="t" if pi == 2 else None)
                        dst = ar.f32("rstg")[:, eg * 512:(eg + 1) * 512]
                        ewaits = [("t", t_op)] + ([("sg", rstg_free)] if eg == 0 else [])
                        ev = pl.op("scalar", lambda e, dst=dst, s_=s_, ob=ob: e.activation(
                            out=dst, in_=PS[ob][:, :], func=AF.Copy, scale=tiny(T_RRC + s_)),
                            waits=ewaits, inc="a")
                        ev_list.append(("a", ev))
                        opj_free[ob] = [ev_list[-1]]
                        if ob == 0:
                            last_b0 = ("a", ev)
                        else:
                            bank_free[5] = [ev_list[-1]]
                    rdst = rs_in[q0 + s_ * 128:q0 + (s_ + 1) * 128, :]
                    rstg_free = pl.dma("gpsimd", rdst, ar.f32("rstg"), waits=ev_list,
                                       accum=(hidx > 0))
                if last_b0 is not None:
                    sc_ev[0] = sc_ev[0] + [last_b0]

        for hidx in range(A_HEADS):
            phase_a1(hidx)
            pl.barrier()
            if hidx == 0 and "vpre" in dbg:
                vp = dbg["vpre"].rearrange("(n p) s -> n p s", p=128)
                pl.dma("gpsimd", vp[0], ar.bf16("Vth"))
                pl.dma("gpsimd", vp[1], ar.bf16("Vtl"))
                pl.dma("gpsimd", dbg["vtf"], ar.f32("vtf"))
                if "w32d" in dbg:
                    pl.dma("gpsimd", dbg["w32d"], ar.f32("wq32")[:, 2 * 2048:3 * 2048])
                    pl.dma("gpsimd", dbg["whd"], ar.bf16("wqh")[:, 2 * 2048:3 * 2048])
                if "ps5" in dbg:
                    a_d = pl.op("scalar", lambda e: e.activation(
                        out=ar.f32("rstg")[:, 0:512], in_=PS[3][:, :], func=AF.Copy),
                        inc="a")
                    pl.dma("sync", dbg["ps5"], ar.f32("rstg")[:, 0:512],
                           waits=[("a", a_d)])
                pl.barrier()
            phase_a2(hidx, hidx)
            pl.barrier()

        if "rsin" in dbg:
            pl.dma("gpsimd", dbg["rsin"], rs_in)
            pl.barrier()
        if "ssb" in dbg:
            pl.dma("gpsimd", dbg["ssb"], ar.f32("S_sb"))
            pl.dma("gpsimd", dbg["psb"], ar.f32("P_sb"))
            pl.dma("gpsimd", dbg["pt"], ar.bf16("PT"))
            oh2 = dbg["oh"].rearrange("p (n c) -> n p c", c=256)
            pl.dma("gpsimd", oh2[0], ar.bf16("Oh"))
            pl.dma("gpsimd", oh2[1], ar.bf16("Ol"))
            pl.dma("gpsimd", dbg["rstg"], ar.f32("rstg"))
            pl.dma("gpsimd", dbg["tiny"], ar.f32("tiny"))
            pl.barrier()
        if "kvq" in dbg:
            kv = dbg["kvq"].rearrange("(n p) s -> n p s", p=128)
            for j, nm in enumerate(("KTh", "KTl", "QTh", "QTl", "Vth", "Vtl")):
                pl.dma("gpsimd", kv[j], ar.bf16(nm))
            pl.barrier()

        pl.wait("gpsimd", ("sg", pl.cnt["sg"]))
        pl.op("gpsimd", lambda e: e.collective_compute(
            "ReduceScatter", ALU.add,
            replica_groups=[[0, 1, 2, 3], [4, 5, 6, 7]],
            ins=[rs_in.opt()], outs=[rs_out.opt()]), inc="cc", real=False)

    if "F" in PHASES:
        ar.reset(g_base)
        ar.alloc("xoh", 8192)
        ar.alloc("xol", 8192)
        ar.alloc("hT", 256)
        ar.alloc("gusl", 4096)
        ar.alloc("wps", 2048)
        ar.alloc("gusl1", 4096)
        ar.alloc("wps1", 2048)
        ar.alloc("hbuf", 256)
        ar.alloc("hsil", 512)
        ar.alloc("sqs", 256)
        ar.alloc("pacc", 16384)
        ar.alias("xbufF0", "pacc", 0, D)
        ar.alias("xbufF1", "pacc", D, D)
        ar.alias("xnfF", "pacc", 2 * D, D)
        ar.alias("sqscrF", "pacc", 3 * D, D)
        ar.alias("ftrh", "pacc", 4 * D, 1024)
        ar.alias("ftrl", "pacc", 4 * D + 1024, 1024)

        xoh4 = ar.bf16("xoh").rearrange("p (dc t) -> p dc t", t=1024)
        xol4 = ar.bf16("xol").rearrange("p (dc t) -> p dc t", t=1024)

        def f_writer(i, a_hi, v_lo):
            src_h = ar.bf16("ftrh").rearrange("p (dc t) -> p dc t", t=128)
            pl.op("vector", lambda e, i=i, src_h=src_h: e.tensor_copy(
                out=xoh4[:, :, i * 128:(i + 1) * 128], in_=src_h),
                waits=[("a", a_hi)])
            src_l = ar.bf16("ftrl").rearrange("p (dc t) -> p dc t", t=128)
            v_c = pl.op("vector", lambda e, i=i, src_l=src_l: e.tensor_copy(
                out=xol4[:, :, i * 128:(i + 1) * 128], in_=src_l), inc="v")
            return ("v", v_c)

        emit_ln_tr_phase(x_own, 8, f_writer, "xnfF", "sqscrF",
                         "xbufF0", "xbufF1", ar.bf16("ftrh"), ar.bf16("ftrl"))
        pl.barrier()
        pl.wait("sync", ("cc", 1 if "Q" in PHASES else 0))

        for sb_ in range(8):
            pl.op("vector", lambda e, sb_=sb_: e.memset(
                ar.f32("pacc")[:, sb_ * 2048:(sb_ + 1) * 2048], 0.0))
            last = (sb_ == 7)
            pl.op("vector", lambda e, sb_=sb_: e.memset(tiny(T_SSQ + sb_), 0.0),
                  inc="v" if last else None, flush=True)
        v_zero = pl.cnt["v"]

        gusl4P = [ar.bf16("gusl").rearrange("p (dc w) -> p dc w", w=512),
                  ar.bf16("gusl1").rearrange("p (dc w) -> p dc w", w=512)]
        wps4P = [ar.bf16("wps").rearrange("p (hc e) -> p hc e", e=2048),
                 ar.bf16("wps1").rearrange("p (hc e) -> p hc e", e=2048)]
        agg = ag_out.rearrange("(rb r) c -> rb r c", rb=NCORE)
        gu_bank_free = {0: [], 1: []}
        slab_freeP = {0: 0, 1: 0}
        sd_prevF = 0
        pj_free = {3: [], 4: []}
        hT_free = 0
        hT4 = ar.bf16("hT").rearrange("p (hc hl c) -> p hc hl c", hc=2, hl=2)
        for hg in range(32):
            par = hg % 2
            gusl4 = gusl4P[par]
            wps4 = wps4P[par]
            for j, hcol in enumerate((256 * hg, HF + 256 * hg)):
                rb, cl = hcol // 2048, hcol % 2048
                src = agg[rb, 0:2048, cl:cl + 256].rearrange("(dc p) w -> p dc w", p=128)
                pl.dma("sync", gusl4[:, :, j * 256:(j + 1) * 256], src,
                       waits=[("t", slab_freeP[par]), ("sd", sd_prevF)] if j == 0 else ())
            for rb in range(NCORE):
                pTr = agg[rb, 2048:AGR, :].rearrange("r c -> (r c)").rearrange("(h e) -> h e", e=256)
                src = pTr[256 * hg:256 * (hg + 1), :].rearrange("(hc p) e -> p hc e", p=128)
                pl.dma("sync", wps4[:, :, rb * 256:(rb + 1) * 256], src)
            sd_sl = pl.cnt["sd"]
            sd_prevF = sd_sl
            for sub in range(8):
                gb = sub % 2
                pl.wait("tensor", *gu_bank_free[gb])
                gu_bank_free[gb] = []
                pl.wait("tensor", ("sd", sd_sl))
                t_gu = None
                first = True
                for dc in range(16):
                    for xi, xs in ((0, xoh4), (1, xol4)):
                        lt = xs[:, dc, sub * 128:(sub + 1) * 128]
                        rt = gusl4[:, dc, :]
                        last = (dc == 15) and (xi == 1)
                        t_gu = pl.op("tensor", lambda e, lt=lt, rt=rt, gb=gb, first=first, last=last: e.matmul(
                            PS[gb][:, :], lt, rt, start=first, stop=last),
                            inc="t" if last else None)
                        first = False
                pl.op("scalar", lambda e: e.preload_act_table(AF.Silu))
                a_sil = pl.op("scalar", lambda e, gb=gb: e.activation(
                    out=ar.f32("hsil"), in_=PS[gb][:, 0:512], func=AF.Silu),
                    waits=[("t", t_gu), ("v", pl.cnt["v"])], inc="a")
                v_h = pl.op("vector", lambda e, gb=gb: e.tensor_tensor(
                    out=ar.f32("hbuf"), in0=ar.f32("hsil")[:, 0:256], in1=PS[gb][:, 256:512], op=ALU.mult),
                    waits=[("a", a_sil)], inc="v")
                gu_bank_free[gb] = [("v", v_h)]
                pl.op("vector", lambda e: e.tensor_tensor(
                    out=ar.f32("sqs"), in0=ar.f32("hbuf"), in1=ar.f32("hbuf"), op=ALU.mult))
                rsl = T_SQC + sub % 2
                pl.op("vector", lambda e, rsl=rsl: e.tensor_reduce(
                    out=tiny(rsl), in_=ar.f32("sqs"), axis=AX.X, op=ALU.add), flush=True)
                pl.op("vector", lambda e, sub=sub, rsl=rsl: e.tensor_tensor(
                    out=tiny(T_SSQ + sub), in0=tiny(T_SSQ + sub), in1=tiny(rsl), op=ALU.add),
                    waits=[("v", v_zero)], flush=True)
                mm_bank_wait(2)
                pl.wait("tensor", ("v", pl.cnt["v"]), ("t", hT_free))
                t_ht = None
                for j in range(2):
                    t_ht = pl.op("tensor", lambda e, j=j: e.transpose(
                        PS[2][:, j * 128:(j + 1) * 128], ar.f32("hbuf")[:, j * 128:(j + 1) * 128], ident),
                        inc="t" if j == 1 else None)
                src2 = PS[2][:, 0:256].rearrange("p (hc c) -> p hc c", c=128)
                a_hh = pl.op("scalar", lambda e, src2=src2: e.activation(
                    out=hT4[:, :, 0, :], in_=src2, func=AF.Copy),
                    waits=[("t", t_ht)], inc="a")
                v_hl = pl.op("vector", lambda e, src2=src2: e.tensor_tensor(
                    out=hT4[:, :, 1, :], in0=src2, in1=hT4[:, :, 0, :], op=ALU.subtract),
                    waits=[("a", a_hh)], inc="v")
                bank_free[2] = [("v", v_hl)]
                for eg in range(4):
                    pjb = 3 + eg % 2
                    pl.wait("tensor", *pj_free[pjb])
                    pj_free[pjb] = []
                    pl.wait("tensor", ("a", a_hh), ("v", v_hl))
                    t_pj = None
                    first = True
                    for hc in range(2):
                        for hl in range(2):
                            lt = hT4[:, hc, hl, :]
                            rt = wps4[:, hc, eg * 512:(eg + 1) * 512]
                            last = (hc == 1) and (hl == 1)
                            t_pj = pl.op("tensor", lambda e, lt=lt, rt=rt, pjb=pjb, first=first, last=last: e.matmul(
                                PS[pjb][:, :], lt, rt, start=first, stop=last),
                                inc="t" if last else None)
                            first = False
                    dst = ar.f32("pacc")[:, sub * 2048 + eg * 512: sub * 2048 + (eg + 1) * 512]
                    v_pa = pl.op("vector", lambda e, pjb=pjb, dst=dst: e.tensor_tensor(
                        out=dst, in0=dst, in1=PS[pjb][:, :], op=ALU.add),
                        waits=[("t", t_pj)], inc="v")
                    pj_free[pjb] = [("v", v_pa)]
                hT_free = pl.cnt["t"]
            slab_freeP[par] = pl.cnt["t"]
        pl.barrier()

        G_OFF = ar.named["xoh"][0]
        gbuf = ar.h[:, G_OFF:G_OFF + 16384]
        a_g = 0
        for sub in range(8):
            pl.op("vector", lambda e, sub=sub: e.tensor_scalar(
                out=tiny(T_VAR), in0=tiny(T_SSQ + sub), scalar1=1.0 / HF,
                scalar2=RMS_EPS, op0=ALU.mult, op1=ALU.add),
                waits=[("a", pl.cnt["a"])], flush=True)
            v_rs = emit_rsqrt(T_VAR, T_FIN + sub, T_TMP)
            pl.wait("scalar", ("v", v_rs))
            pl.op("scalar", lambda e: e.preload_act_table(AF.Silu))
            a_g = pl.op("scalar", lambda e, sub=sub: e.activation(
                out=gbuf[:, sub * 2048:(sub + 1) * 2048],
                in_=ar.f32("pacc")[:, sub * 2048:(sub + 1) * 2048],
                func=AF.Silu, scale=tiny(T_FIN + sub)), inc="a")
        if "gbuf" in dbg:
            for sub in range(8):
                pl.dma("sync", dbg["gbuf"][sub * 128:(sub + 1) * 128, :],
                       gbuf[:, sub * 2048:(sub + 1) * 2048], waits=[("a", a_g)])
            pl.barrier()

    if "C" in PHASES:
        X_OFF = ar.named["pacc"][0] if "F" in PHASES else g_base
        xbC = ar.h[:, X_OFF:X_OFF + 2048]
        sbC = ar.h[:, X_OFF + 2048:X_OFF + 4096]
        G_OFF2 = ar.named["xoh"][0] if "F" in PHASES else g_base + 4096
        gbuf2 = ar.h[:, G_OFF2:G_OFF2 + 16384]
        pl.barrier()
        if "A" in PHASES:
            pl.wait("sync", ("cc", 2 if "Q" in PHASES else 1))
        if "rsout" in dbg:
            pl.dma("sync", dbg["rsout"], rs_out)
        so_fin = 0
        for sub in range(8):
            vdone = ([("v", pl.cnt["v"]), ("sd", so_fin)] if sub >= 1 else [])
            sd_xc = pl.dma("sync", xbC, x_own[sub * 128:(sub + 1) * 128, :], waits=vdone)
            sd_sc = pl.dma("sync", sbC, rs_out[sub * 128:(sub + 1) * 128, :])
            g_sl = gbuf2[:, sub * 2048:(sub + 1) * 2048]
            pl.op("vector", lambda e, g_sl=g_sl: e.tensor_tensor(
                out=sbC, in0=sbC, in1=g_sl, op=ALU.mult),
                waits=[("sd", sd_sc), ("a", pl.cnt["a"])])
            v_o = pl.op("vector", lambda e: e.tensor_tensor(
                out=sbC, in0=sbC, in1=xbC, op=ALU.add),
                waits=[("sd", sd_xc)], inc="v")
            so_fin = pl.dma("sync", out_c[sub * 128:(sub + 1) * 128, :], sbC,
                            waits=[("v", v_o)])
        pl.wait("sync", ("sd", so_fin))
        pl.wait("gpsimd", ("sg", pl.cnt["sg"]))

    pl.check()

    import contextlib
    with contextlib.ExitStack() as stack:
        SM = {}
        for s_ in ("sd", "sg", "sv", "sa", "t", "v", "a", "g", "cc", "bar"):
            SM[s_] = stack.enter_context(nc.semaphore(s_))
        block = stack.enter_context(nc.Block())
        for eng in ENGS:
            def body(e, eng=eng):
                for f in pl.ops[eng]:
                    f(e, SM)
            getattr(block, eng)(body)
    return nc


def _host_inputs(L0_attn_x, L0_attn_W_qkv, L0_attn_W_out, L0_ffn_w_gate_up, L0_ffn_w_proj):
    x = np.ascontiguousarray(np.asarray(L0_attn_x, dtype=np.float32))
    wqkv = np.asarray(L0_attn_W_qkv, dtype=np.float32)
    wout = np.asarray(L0_attn_W_out, dtype=np.float32)
    wgu = np.ascontiguousarray(np.asarray(L0_ffn_w_gate_up, dtype=np.float32))
    wp = np.ascontiguousarray(np.asarray(L0_ffn_w_proj, dtype=np.float32))
    wqkvT = np.ascontiguousarray(wqkv.T)
    woutT = np.ascontiguousarray(wout.T)
    ident = np.eye(128, dtype=np.float32)
    masks = np.zeros((128, 4 * 512), dtype=np.float32)
    for j in range(4):
        p = np.arange(128)[:, None]
        f = np.arange(512)[None, :]
        masks[:, j * 512:(j + 1) * 512] = np.where(f <= 128 * j + p, 0.0, -1.0e5)
    masks = masks.astype(ml_dtypes.bfloat16)
    in_maps = []
    for c in range(NCORE):
        seq, l = c // 4, c % 4
        h0 = 4 * l
        qcols = np.arange(h0 * 128, (h0 + 4) * 128)
        cols = np.concatenate([qcols, 2048 + qcols, 4096 + qcols])
        in_maps.append({
            "x_seq": x[seq],
            "x_own": np.ascontiguousarray(x[seq, l * OWN:(l + 1) * OWN]),
            "wqkvT": np.ascontiguousarray(wqkvT[:, cols]),
            "woutT": np.ascontiguousarray(woutT[h0 * 128:(h0 + 4) * 128, :]),
            "wgu_sh": np.ascontiguousarray(wgu[c * (GU // NCORE):(c + 1) * (GU // NCORE)]),
            "wp_sh": np.ascontiguousarray(wp[c * (D // NCORE):(c + 1) * (D // NCORE)]),
            "ident_in": ident,
            "masks_in": masks,
        })
    return in_maps


def kernel(L0_attn_x, L0_attn_W_qkv, L0_attn_W_out, L0_ffn_w_gate_up, L0_ffn_w_proj):
    in_maps = _host_inputs(L0_attn_x, L0_attn_W_qkv, L0_attn_W_out,
                           L0_ffn_w_gate_up, L0_ffn_w_proj)
    nc = build_core_program()
    res = run_bass_kernel_spmd(nc, in_maps, core_ids=list(range(NCORE)))
    out = np.empty((2, S, D), dtype=np.float32)
    for c in range(NCORE):
        seq, l = c // 4, c % 4
        out[seq, l * OWN:(l + 1) * OWN] = np.asarray(res.results[c]["out_c"])
    return out


# version 04 (final)
# speedup vs baseline: 1.0053x; 1.0053x over previous
import math
import numpy as np
import ml_dtypes
import concourse.bass as bass
import concourse.mybir as mybir
from concourse.bass_utils import run_bass_kernel_spmd

F32 = mybir.dt.float32
BF16 = mybir.dt.bfloat16
ALU = mybir.AluOpType
AF = mybir.ActivationFunctionType
AX = mybir.AxisListType

D = 2048
S = 4096
NCORE = 8
NHC = 4
DK = 128
HF = 8192
GU = 2 * HF
OWN = 1024
QTS = 256
INVSQ = float(1.0 / math.sqrt(128.0))
LN_EPS = 1e-5
RMS_EPS = float(np.float32(1.1920929e-7))
MAG = 192.0
KCMAX = 34
AGR = 3072

ENGS = ("sync", "tensor", "vector", "scalar", "gpsimd")
OWN_SEM = {"tensor": "t", "vector": "v", "scalar": "a", "gpsimd": "g"}

PHASES = ("P", "Q", "A", "F", "C")
DBG_DUMPS = ()
P_TILES = 32
A_HEADS = NHC if 'NHC' in dir() else 4


class Planner:
    def __init__(self):
        self.ops = {e: [] for e in ENGS}
        self.cnt = {}
        self.maxwait = {}
        self.uninc = {e: 0 for e in ENGS}

    def sem(self, *names):
        for n in names:
            self.cnt[n] = 0
            self.maxwait[n] = 0

    def op(self, eng, fn, waits=(), inc=None, amt=1, real=True, flush=False):
        if flush:
            own = OWN_SEM[eng]
            waits = tuple(waits) + ((own, self.cnt[own]),)
            inc = inc or own
        w = tuple((s, v) for (s, v) in waits if v > 0)
        for s, v in w:
            assert s in self.cnt, s
            self.maxwait[s] = max(self.maxwait[s], v)
        if inc is not None:
            self.cnt[inc] += amt
        if eng in OWN_SEM:
            if real:
                self.uninc[eng] += 1
                if inc == OWN_SEM[eng]:
                    self.uninc[eng] = 0

        def thunk(e, SM, fn=fn, w=w, inc=inc, amt=amt):
            for s, v in w:
                e.wait_ge(SM[s], v)
            r = fn(e)
            if inc is not None:
                r.then_inc(SM[inc], amt)
        self.ops[eng].append(thunk)
        return self.cnt[inc] if inc is not None else None

    def wait(self, eng, *pairs):
        w = tuple((s, v) for (s, v) in pairs if v > 0)
        if not w:
            return
        for s, v in w:
            self.maxwait[s] = max(self.maxwait[s], v)

        def thunk(e, SM, w=w):
            for s, v in w:
                e.wait_ge(SM[s], v)
        self.ops[eng].append(thunk)

    def dma(self, queue, out, in_, waits=(), accum=False):
        sem = {"sync": "sd", "gpsimd": "sg", "vector": "sv", "scalar": "sa"}[queue]
        if accum:
            return self.op(queue, lambda e: e.dma_start(out=out, in_=in_, accum_op=ALU.add),
                           waits=waits, inc=sem, amt=16, real=False)
        return self.op(queue, lambda e: e.dma_start(out=out, in_=in_),
                       waits=waits, inc=sem, amt=16, real=False)

    def barrier(self):
        for e in ("tensor", "vector", "scalar", "gpsimd"):
            assert self.uninc[e] == 0, f"barrier: engine {e} has {self.uninc[e]} un-inc'd ops"
        drains = (("sd", self.cnt["sd"]), ("sg", self.cnt["sg"]),
                  ("sv", self.cnt["sv"]), ("sa", self.cnt["sa"]))
        base = self.cnt["bar"]
        for e in ENGS:
            own = (((OWN_SEM[e], self.cnt[OWN_SEM[e]]),) if e in OWN_SEM else ())
            self.op(e, lambda en: en.nop(), waits=drains + own, inc="bar", real=False)
        for e in ENGS:
            self.wait(e, ("bar", base + len(ENGS)))

    def check(self):
        for s, v in self.maxwait.items():
            assert v <= self.cnt[s], f"sem {s}: wait {v} > final {self.cnt[s]}"


class Arena:
    NCOLS = 50176

    def __init__(self, handle):
        self.ptr = 0
        self.named = {}
        self.h = handle

    def alloc(self, name, cols):
        assert self.ptr + cols <= self.NCOLS, f"{name}: OOM {self.ptr}+{cols}"
        self.named[name] = (self.ptr, cols)
        self.ptr += cols

    def alias(self, name, base_name, rel_off, cols):
        o, n = self.named[base_name]
        assert rel_off + cols <= n, f"alias {name} exceeds {base_name}"
        self.named[name] = (o + rel_off, cols)

    def mark(self):
        return self.ptr

    def reset(self, p):
        self.ptr = p

    def f32(self, name):
        o, n = self.named[name]
        return self.h[:, o:o + n]

    def bf16(self, name):
        o, n = self.named[name]
        return self.h[:, o:o + n].bitcast(BF16)


def build_core_program():
    nc = bass.Bass("TRN2", target_bir_lowering=False, detect_race_conditions=False)
    pl = Planner()
    pl.sem("sd", "sg", "sv", "sa", "t", "v", "a", "g", "cc", "bar")

    x_seq = nc.dram_tensor("x_seq", [S, D], F32, kind="ExternalInput").ap()
    wqkvT = nc.dram_tensor("wqkvT", [D, 3 * NHC * DK], F32, kind="ExternalInput").ap()
    woutT = nc.dram_tensor("woutT", [NHC * DK, D], F32, kind="ExternalInput").ap()
    wgu_sh = nc.dram_tensor("wgu_sh", [GU // NCORE, D], F32, kind="ExternalInput").ap()
    wp_sh = nc.dram_tensor("wp_sh", [D // NCORE, HF], F32, kind="ExternalInput").ap()
    ident_in = nc.dram_tensor("ident_in", [128, 128], F32, kind="ExternalInput").ap()
    masks_in = nc.dram_tensor("masks_in", [128, 4 * 512], BF16, kind="ExternalInput").ap()
    x_own = nc.dram_tensor("x_own", [OWN, D], F32, kind="ExternalInput").ap()
    out_c = nc.dram_tensor("out_c", [OWN, D], F32, kind="ExternalOutput").ap()

    xsp_h = nc.dram_tensor("xsp_h", [D, S], BF16, kind="Internal").ap()
    xsp_l = nc.dram_tensor("xsp_l", [D, S], BF16, kind="Internal").ap()
    rs_in = nc.dram_tensor("rs_in", [S, D], F32, kind="Internal").ap()
    rs_out = nc.dram_tensor("rs_out", [OWN, D], F32, kind="Internal").ap()
    ag_in = nc.dram_tensor("ag_in", [AGR, D], BF16, kind="Internal").ap()
    ag_out = nc.dram_tensor("ag_out", [NCORE * AGR, D], BF16, kind="Internal",
                            addr_space="Shared").ap()

    dbg = {}
    for name in DBG_DUMPS:
        shapes = {"xsp": [D, S], "ag": [NCORE * AGR, D], "kvq": [6 * 128, S],
                  "rsin": [S, D], "rsout": [OWN, D], "gbuf": [OWN, D],
                  "xnf": [128, D], "tiny": [128, 64],
                  "ssb": [128, S], "psb": [128, S], "pt": [128, KCMAX * 512],
                  "oh": [128, 512], "rstg": [128, D],
                  "vtf": [128, 256], "vpre": [2 * 128, S], "ps5": [128, 512],
                  "w32d": [128, 2048], "whd": [128, 2048]}
        dt = {"xsp": BF16, "ag": BF16, "kvq": BF16, "rsin": F32, "rsout": F32,
              "gbuf": F32, "xnf": F32, "tiny": F32,
              "ssb": F32, "psb": F32, "pt": BF16, "oh": BF16, "rstg": F32,
              "vtf": F32, "vpre": BF16, "ps5": F32, "w32d": F32, "whd": BF16}[name]
        dbg[name] = nc.dram_tensor("dbg_" + name, shapes[name], dt, kind="ExternalOutput").ap()

    arena_h = nc.alloc_sbuf_tensor("arena", [128, Arena.NCOLS], F32)
    ar = Arena(arena_h.ap())
    PS = [nc.alloc_psum_tensor(f"ps{i}", [128, 512], F32).ap() for i in range(8)]

    ar.alloc("ident", 128)
    ar.alloc("masks", 1024)
    ar.alloc("tiny", 64)
    T_SUMX, T_SUMSQ, T_MEAN, T_VAR, T_INV, T_NMI, T_TMP = 0, 1, 2, 3, 4, 5, 6
    T_NGM, T_NGC, T_RSR, T_RSC, T_RRC = 8, 10, 12, 14, 16
    T_SSQ, T_SQC, T_FIN = 24, 32, 40

    def tiny(slot):
        o, _ = ar.named["tiny"]
        return ar.h[:, o + slot:o + slot + 1]

    g_base = ar.mark()
    ident = ar.f32("ident")

    pl.dma("sync", ident, ident_in)
    sd_const = pl.dma("sync", ar.bf16("masks"), masks_in)

    bank_free = {i: [] for i in range(8)}

    def mm_bank_wait(bank):
        pl.wait("tensor", *bank_free[bank])
        bank_free[bank] = []

    def emit_rsqrt(x_slot, out_slot, s_slot):
        pl.op("vector", lambda e: e.tensor_scalar(
            out=tiny(s_slot), in0=tiny(x_slot), scalar1=0.5, scalar2=0.5,
            op0=ALU.mult, op1=ALU.add), flush=True)
        for _ in range(7):
            pl.op("vector", lambda e: e.reciprocal(out=tiny(out_slot), in_=tiny(s_slot)), flush=True)
            pl.op("vector", lambda e: e.tensor_tensor(
                out=tiny(out_slot), in0=tiny(x_slot), in1=tiny(out_slot), op=ALU.mult), flush=True)
            pl.op("vector", lambda e: e.tensor_tensor(
                out=tiny(s_slot), in0=tiny(s_slot), in1=tiny(out_slot), op=ALU.add), flush=True)
            pl.op("vector", lambda e: e.tensor_scalar_mul(
                out=tiny(s_slot), in0=tiny(s_slot), scalar1=0.5), flush=True)
        return pl.op("vector", lambda e: e.reciprocal(
            out=tiny(out_slot), in_=tiny(s_slot)), flush=True)

    def emit_ln(xb, xnf, sqscr, sd_x, t_guard):
        v_guard = pl.cnt["v"]
        pl.op("vector", lambda e: e.tensor_reduce(
            out=tiny(T_SUMX), in_=xb, axis=AX.X, op=ALU.add), waits=[("sd", sd_x)],
            flush=True)
        pl.op("scalar", lambda e: e.preload_act_table(AF.Square))
        a_sq = pl.op("scalar", lambda e: e.activation(
            out=sqscr, in_=xb, func=AF.Square, accum_out=tiny(T_SUMSQ)),
            waits=[("sd", sd_x), ("v", v_guard)], inc="a")
        pl.op("vector", lambda e: e.tensor_scalar_mul(out=tiny(T_MEAN), in0=tiny(T_SUMX), scalar1=1.0 / D),
              flush=True)
        pl.op("vector", lambda e: e.tensor_scalar_mul(out=tiny(T_VAR), in0=tiny(T_SUMSQ), scalar1=1.0 / D),
              waits=[("a", a_sq)], flush=True)
        pl.op("vector", lambda e: e.tensor_tensor(out=tiny(T_TMP), in0=tiny(T_MEAN), in1=tiny(T_MEAN), op=ALU.mult),
              flush=True)
        pl.op("vector", lambda e: e.tensor_tensor(out=tiny(T_VAR), in0=tiny(T_VAR), in1=tiny(T_TMP), op=ALU.subtract),
              flush=True)
        pl.op("vector", lambda e: e.tensor_scalar_add(out=tiny(T_VAR), in0=tiny(T_VAR), scalar1=LN_EPS),
              flush=True)
        emit_rsqrt(T_VAR, T_INV, T_TMP)
        pl.op("vector", lambda e: e.tensor_scalar_mul(out=tiny(T_NMI), in0=tiny(T_MEAN), scalar1=-1.0),
              flush=True)
        v_xnf = pl.op("vector", lambda e: e.tensor_scalar(
            out=xnf, in0=xb, scalar1=tiny(T_NMI), scalar2=tiny(T_INV),
            op0=ALU.add, op1=ALU.mult),
            waits=[("t", t_guard)], inc="v", flush=True)
        return a_sq, v_xnf

    def emit_ln_tr_phase(x_src, n_tiles, writer, xnf_name, sq_name,
                         xb0_name, xb1_name, hi_stg, lo_stg):
        done_xb = [(0, 0), (0, 0)]
        wr_guard = (None, 0)
        for i in range(n_tiles):
            bsel = i % 2
            xb = ar.f32(xb0_name) if bsel == 0 else ar.f32(xb1_name)
            sd_x = pl.dma("sync", xb, x_src[i * 128:(i + 1) * 128, :],
                          waits=[("a", done_xb[bsel][0]), ("v", done_xb[bsel][1])])
            a_sq, v_xnf = emit_ln(xb, ar.f32(xnf_name), ar.f32(sq_name), sd_x, pl.cnt["t"])
            done_xb[bsel] = (a_sq, v_xnf)
            pb = 0 if bsel == 0 else 4
            bw = []
            for b in range(4):
                bw += bank_free[pb + b]
                bank_free[pb + b] = []
            pl.wait("tensor", ("v", v_xnf), ("sd", sd_const), *bw)
            t_tr = None
            for dc in range(16):
                bank, col = pb + dc // 4, (dc % 4) * 128
                t_tr = pl.op("tensor", lambda e, dc=dc, bank=bank, col=col: e.transpose(
                    PS[bank][:, col:col + 128],
                    ar.f32(xnf_name)[:, dc * 128:(dc + 1) * 128], ident),
                    inc="t" if dc == 15 else None)
            a_hi = v_lo = 0
            for b in range(4):
                ew = [("t", t_tr)]
                if b == 0 and wr_guard[0] is not None:
                    ew.append(wr_guard)
                a_hi = pl.op("scalar", lambda e, pb=pb, b=b: e.activation(
                    out=hi_stg[:, b * 512:(b + 1) * 512], in_=PS[pb + b][:, :], func=AF.Copy),
                    waits=ew, inc="a")
                v_lo = pl.op("vector", lambda e, pb=pb, b=b: e.tensor_tensor(
                    out=lo_stg[:, b * 512:(b + 1) * 512], in0=PS[pb + b][:, :],
                    in1=hi_stg[:, b * 512:(b + 1) * 512], op=ALU.subtract),
                    waits=[("a", a_hi)], inc="v")
                bank_free[pb + b] = [("v", v_lo)]
            wr_guard = writer(i, a_hi, v_lo)

    if "P" in PHASES:
        ar.reset(g_base)
        ar.alloc("xbuf0", D)
        ar.alloc("xbuf1", D)
        ar.alloc("sqscr", D)
        ar.alloc("xnf", D)
        ar.alloc("trsh", 1024)
        ar.alloc("trsl", 1024)

        def p_writer(i, a_hi, v_lo):
            src_h = ar.bf16("trsh").rearrange("p (dc t) -> p dc t", t=128)
            dst_h = xsp_h.rearrange("(dc p) (i t) -> p dc i t", p=128, t=128)[:, :, i, :]
            pl.dma("sync", dst_h, src_h, waits=[("a", a_hi), ("v", v_lo)])
            src_l = ar.bf16("trsl").rearrange("p (dc t) -> p dc t", t=128)
            dst_l = xsp_l.rearrange("(dc p) (i t) -> p dc i t", p=128, t=128)[:, :, i, :]
            sd_w = pl.dma("sync", dst_l, src_l)
            return ("sd", sd_w)

        emit_ln_tr_phase(x_seq, P_TILES, p_writer, "xnf", "sqscr",
                         "xbuf0", "xbuf1", ar.bf16("trsh"), ar.bf16("trsl"))
        pl.barrier()
        if "xnf" in dbg:
            pl.dma("gpsimd", dbg["xnf"], ar.f32("xnf"))
        if "tiny" in dbg:
            pl.dma("gpsimd", dbg["tiny"], ar.f32("tiny"))
        if "xsp" in dbg:
            pl.dma("gpsimd", dbg["xsp"], xsp_h)
        if dbg:
            pl.barrier()

    if "Q" in PHASES:
        ar.reset(g_base)
        ar.alloc("qwf", HF)
        ar.alloc("qwb", HF // 2)
        ar.alloc("qratio", HF // 2)
        ar.alloc("qsc", 128)
        ar.alloc("qscb", 64)
        ar.alloc("qrec", 128)
        ar.alloc("qst", 512)
        qst_free = 0
        tile_free = 0

        def quant_tile(src_ap, w, ngrp, dst_writer):
            nonlocal qst_free, tile_free
            sg_b = pl.dma("gpsimd", ar.bf16("qwb")[:, :w], src_ap, waits=[("t", tile_free)])
            qwb_g = ar.bf16("qwb")[:, :w].rearrange("p (g k) -> p g k", k=64)
            pl.op("vector", lambda e: e.tensor_reduce(
                out=ar.f32("qsc")[:, :ngrp], in_=qwb_g, axis=AX.X, op=ALU.add,
                apply_absolute_value=True), waits=[("sg", sg_b), ("t", tile_free)],
                flush=True)
            pl.op("vector", lambda e: e.tensor_scalar(
                out=ar.bf16("qscb")[:, :ngrp], in0=ar.f32("qsc")[:, :ngrp],
                scalar1=1.0 / 64.0, scalar2=1e-8, op0=ALU.mult, op1=ALU.max),
                flush=True)
            pl.op("vector", lambda e: e.tensor_copy(
                out=ar.f32("qrec")[:, :ngrp], in_=ar.bf16("qscb")[:, :ngrp]),
                flush=True)
            pl.op("vector", lambda e: e.reciprocal(
                out=ar.f32("qrec")[:, :ngrp], in_=ar.f32("qrec")[:, :ngrp]),
                flush=True)
            recb = ar.f32("qrec")[:, :ngrp].unsqueeze(2).broadcast_to((128, ngrp, 64))
            sclb = ar.bf16("qscb")[:, :ngrp].unsqueeze(2).broadcast_to((128, ngrp, 64))
            ratio_b = ar.bf16("qratio")[:, :w]
            ratio_g = ratio_b.rearrange("p (g k) -> p g k", k=64)
            pl.op("vector", lambda e: e.tensor_tensor(out=ratio_g, in0=qwb_g, in1=recb, op=ALU.mult),
                flush=True)
            pl.op("vector", lambda e: e.tensor_scalar_add(out=ratio_b, in0=ratio_b, scalar1=MAG))
            pl.op("vector", lambda e: e.tensor_scalar(
                out=ratio_b, in0=ratio_b, scalar1=MAG, scalar2=1.0,
                op0=ALU.subtract, op1=ALU.min))
            v3 = pl.op("vector", lambda e: e.scalar_tensor_tensor(
                out=ar.f32("qwf")[:, :w].rearrange("p (g k) -> p g k", k=64),
                in0=ratio_g, scalar=-1.0, in1=sclb,
                op0=ALU.max, op1=ALU.mult), inc="v")
            nblk = w // 128
            for b0 in range(0, nblk, 8):
                nb = min(8, nblk - b0)
                pl.wait("tensor", ("v", v3), ("sd", qst_free),
                        *(bank_free[0] + bank_free[1]))
                bank_free[0] = []
                bank_free[1] = []
                t_q = None
                for j in range(nb):
                    bank, col = j // 4, (j % 4) * 128
                    t_q = pl.op("tensor", lambda e, j=j, b0=b0, bank=bank, col=col: e.transpose(
                        PS[bank][:, col:col + 128],
                        ar.f32("qwf")[:, (b0 + j) * 128:(b0 + j + 1) * 128], ident),
                        inc="t" if j == nb - 1 else None)
                a_e = pl.op("scalar", lambda e: e.activation(
                    out=ar.bf16("qst")[:, 0:512], in_=PS[0][:, :], func=AF.Copy),
                    waits=[("t", t_q)], inc="a")
                if nb > 4:
                    a_e = pl.op("scalar", lambda e: e.activation(
                        out=ar.bf16("qst")[:, 512:1024], in_=PS[1][:, :], func=AF.Copy),
                        inc="a")
                bank_free[0] = [("a", a_e)]
                bank_free[1] = [("a", a_e)]
                qst_free = dst_writer(b0, nb, a_e)
            tile_free = pl.cnt["t"]

        for ht in range(16):
            def wr_gu(b0, nb, a_e, ht=ht):
                src = ar.bf16("qst")[:, :nb * 128].rearrange("p (j c) -> p j c", c=128)
                dst = ag_in.rearrange("(dc p) (hc c) -> p dc hc c", p=128, c=128)[:, b0:b0 + nb, ht, :]
                return pl.dma("sync", dst, src, waits=[("a", a_e)])
            quant_tile(wgu_sh[ht * 128:(ht + 1) * 128, :], D, 32, wr_gu)
        pT = ag_in[2048:AGR, :].rearrange("r c -> (r c)").rearrange("(h e) -> h e", e=256)
        for et in range(2):
            def wr_p(b0, nb, a_e, et=et):
                src = ar.bf16("qst")[:, :nb * 128].rearrange("p (j c) -> p j c", c=128)
                dst = pT.rearrange("(j p) e -> p j e", p=128)[:, b0:b0 + nb, et * 128:(et + 1) * 128]
                return pl.dma("sync", dst, src, waits=[("a", a_e)])
            quant_tile(wp_sh[et * 128:(et + 1) * 128, :], HF, 128, wr_p)

        pl.barrier()
        pl.op("gpsimd", lambda e: e.collective_compute(
            "AllGather", ALU.bypass, replica_groups=[list(range(NCORE))],
            ins=[ag_in.opt()], outs=[ag_out.opt()]), inc="cc", real=False)
        if "ag" in dbg:
            pl.wait("gpsimd", ("cc", 1))
            pl.dma("gpsimd", dbg["ag"], ag_out)
            pl.barrier()

    if "A" in PHASES:
        ar.reset(g_base)
        ar.alloc("KTh", D)
        ar.alloc("KTl", D)
        ar.alloc("QTh", D)
        ar.alloc("QTl", D)
        ar.alloc("Vth", D)
        ar.alloc("Vtl", D)
        ar.alloc("WOh", D // 2)
        ar.alloc("WOl", D // 2)
        ar.alloc("rstg", D)
        ar.alloc("vtf", 256)
        a_scr = ar.mark()
        ar.alloc("wq32", 16 * 384)
        ar.alloc("wqh", 16 * 384 // 2)
        ar.alloc("wql", 16 * 384 // 2)
        ar.alloc("xsl_h0", 2048)
        ar.alloc("xsl_l0", 2048)
        ar.alloc("xsl_h1", 2048)
        ar.alloc("xsl_l1", 2048)
        a2_base = a_scr

        def phase_a1(h):
            wo_src = woutT[h * 128:(h + 1) * 128, :]
            sd_wo = pl.dma("sync", ar.f32("wq32")[:, :D], wo_src)
            sg_wo = pl.dma("gpsimd", ar.bf16("WOh"), wo_src)
            v_wo = pl.op("vector", lambda e: e.tensor_tensor(
                out=ar.bf16("WOl"), in0=ar.f32("wq32")[:, :D], in1=ar.bf16("WOh"), op=ALU.subtract),
                waits=[("sd", sd_wo), ("sg", sg_wo)], inc="v")
            w32 = ar.f32("wq32").rearrange("p (j dc c) -> p j dc c", j=3, c=128)
            wh = ar.bf16("wqh").rearrange("p (j dc c) -> p j dc c", j=3, c=128)
            wl = ar.bf16("wql").rearrange("p (j dc c) -> p j dc c", j=3, c=128)
            cols = [h * 128, NHC * 128 + h * 128, 2 * NHC * 128 + h * 128]
            sd_w = 0
            for j, c in enumerate(cols):
                src = wqkvT[:, c:c + 128].rearrange("(dc p) e -> p dc e", p=128)
                sd_w = pl.dma("sync", w32[:, j], src,
                              waits=[("v", v_wo)] if j == 0 else ())
            whf, wlf = ar.bf16("wqh"), ar.bf16("wql")
            w32f = ar.f32("wq32")
            pl.op("scalar", lambda e: e.activation(
                out=whf[:, 0:2048], in_=w32f[:, 0:2048], func=AF.Copy, scale=INVSQ),
                waits=[("sd", sd_w)])
            pl.op("scalar", lambda e: e.activation(
                out=whf[:, 2048:4096], in_=w32f[:, 2048:4096], func=AF.Copy))
            a_wh = pl.op("scalar", lambda e: e.activation(
                out=whf[:, 4096:6144], in_=w32f[:, 4096:6144], func=AF.Copy), inc="a")
            pl.op("vector", lambda e: e.scalar_tensor_tensor(
                out=wlf[:, 0:2048], in0=w32f[:, 0:2048], scalar=INVSQ,
                in1=whf[:, 0:2048], op0=ALU.mult, op1=ALU.subtract),
                waits=[("a", a_wh)])
            v_wl = pl.op("vector", lambda e: e.tensor_tensor(
                out=wlf[:, 2048:6144], in0=w32f[:, 2048:6144], in1=whf[:, 2048:6144],
                op=ALU.subtract), inc="v")
            sd_prev = sd_w
            for sp in range(16):
                bsel = sp % 2
                xh = ar.bf16("xsl_h0") if bsel == 0 else ar.bf16("xsl_h1")
                xl = ar.bf16("xsl_l0") if bsel == 0 else ar.bf16("xsl_l1")
                dwaits = [("sd", sd_prev)] + ([("t", pl.cnt["t"])] if sp >= 2 else [])
                xh3 = xh.rearrange("p (dc s) -> p dc s", s=256)
                xl3 = xl.rearrange("p (dc s) -> p dc s", s=256)
                pl.dma("sync", xh3, xsp_h.rearrange("(dc p) s -> p dc s", p=128)[:, :, sp * 256:(sp + 1) * 256],
                       waits=dwaits)
                sd_sl = pl.dma("sync", xl3, xsp_l.rearrange("(dc p) s -> p dc s", p=128)[:, :, sp * 256:(sp + 1) * 256])
                sd_prev = sd_sl
                pb = 0 if bsel == 0 else 3
                pl.wait("tensor", ("sd", sd_sl), ("v", v_wl))
                for bank in (pb, pb + 1, pb + 2):
                    mm_bank_wait(bank)
                t_p = None
                for gi, (proj, bank) in enumerate(((2, pb + 0), (1, pb + 1), (0, pb + 2))):
                    first = True
                    for dc in range(16):
                        whc = wh[:, proj, dc, :]
                        wlc = wl[:, proj, dc, :]
                        xhc = xh3[:, dc, :]
                        xlc = xl3[:, dc, :]
                        pairs = ((whc, xhc), (wlc, xhc), (whc, xlc))
                        for pi, (lt, rt) in enumerate(pairs):
                            last_all = (gi == 2) and (dc == 15) and (pi == 2)
                            t_p = pl.op("tensor", lambda e, lt=lt, rt=rt, bank=bank,
                                        first=first, last=(dc == 15 and pi == 2): e.matmul(
                                PS[bank][:, :256], lt, rt, start=first, stop=last),
                                inc="t" if last_all else None)
                            first = False
                co = sp * 256
                v_vf = pl.op("vector", lambda e, pb=pb: e.tensor_copy(
                    out=ar.f32("vtf"), in_=PS[pb][:, :256]),
                    waits=[("t", t_p)], inc="v")
                bank_free[pb] = [("v", v_vf)]
                a_k = pl.op("scalar", lambda e, pb=pb, co=co: e.activation(
                    out=ar.bf16("KTh")[:, co:co + 256], in_=PS[pb + 1][:, :256], func=AF.Copy),
                    waits=[("t", t_p)], inc="a")
                v_k = pl.op("vector", lambda e, pb=pb, co=co: e.tensor_tensor(
                    out=ar.bf16("KTl")[:, co:co + 256], in0=PS[pb + 1][:, :256],
                    in1=ar.bf16("KTh")[:, co:co + 256], op=ALU.subtract),
                    waits=[("a", a_k)], inc="v")
                bank_free[pb + 1] = [("v", v_k)]
                a_q = pl.op("scalar", lambda e, pb=pb, co=co: e.activation(
                    out=ar.bf16("QTh")[:, co:co + 256], in_=PS[pb + 2][:, :256], func=AF.Copy),
                    inc="a")
                v_q = pl.op("vector", lambda e, pb=pb, co=co: e.tensor_tensor(
                    out=ar.bf16("QTl")[:, co:co + 256], in0=PS[pb + 2][:, :256],
                    in1=ar.bf16("QTh")[:, co:co + 256], op=ALU.subtract),
                    waits=[("a", a_q)], inc="v")
                bank_free[pb + 2] = [("v", v_q)]
                mm_bank_wait(6)
                pl.wait("tensor", ("v", v_vf))
                t_vt = None
                for j in range(2):
                    t_vt = pl.op("tensor", lambda e, j=j: e.transpose(
                        PS[6][:, j * 128:(j + 1) * 128], ar.f32("vtf")[:, j * 128:(j + 1) * 128], ident),
                        inc="t" if j == 1 else None)
                a_vh = pl.op("scalar", lambda e, co=co: e.activation(
                    out=ar.bf16("Vth")[:, co:co + 256], in_=PS[6][:, 0:256], func=AF.Copy),
                    waits=[("t", t_vt)], inc="a")
                v_vl = pl.op("vector", lambda e, co=co: e.tensor_tensor(
                    out=ar.bf16("Vtl")[:, co:co + 256], in0=PS[6][:, 0:256],
                    in1=ar.bf16("Vth")[:, co:co + 256], op=ALU.subtract),
                    waits=[("a", a_vh)], inc="v")
                bank_free[6] = [("v", v_vl)]

        def phase_a2(h, hidx):
            ar.reset(a2_base)
            ar.alloc("S_sb", S)
            ar.alloc("P_sb", S)
            ar.alloc("PT", KCMAX * 256)
            ar.alloc("Oh", 128)
            ar.alloc("Ol", 128)
            sc_ev = {0: [], 1: []}
            pt_ev = {2: [], 3: []}
            rstg_free = 0
            PT_b = ar.bf16("PT").rearrange("p (kc hl sq) -> p kc hl sq", hl=2, sq=256)
            for qi in range(16):
                q0 = QTS * qi
                nch = q0 // 512 + 1
                KC = 2 * qi + 2
                a_guard = pl.cnt["a"]
                pl.op("vector", lambda e: e.memset(tiny(T_NGM + 0), 3e38),
                      waits=[("a", a_guard)], flush=True)
                pl.op("vector", lambda e: e.memset(tiny(T_NGM + 1), 3e38), flush=True)
                pl.op("vector", lambda e: e.memset(tiny(T_RSR + 0), 0.0), flush=True)
                v_init = pl.op("vector", lambda e: e.memset(tiny(T_RSR + 1), 0.0), inc="v",
                               flush=True)
                for s_ in (0, 1):
                    sg_a = pl.cnt["a"]
                    sg_v = pl.cnt["v"]
                    for c in range(nch):
                        bank = c % 2
                        pl.wait("tensor", *sc_ev[bank])
                        sc_ev[bank] = []
                        qh = ar.bf16("QTh")[:, q0 + s_ * 128:q0 + (s_ + 1) * 128]
                        ql = ar.bf16("QTl")[:, q0 + s_ * 128:q0 + (s_ + 1) * 128]
                        kh = ar.bf16("KTh")[:, c * 512:(c + 1) * 512]
                        kl = ar.bf16("KTl")[:, c * 512:(c + 1) * 512]
                        t_sc = None
                        for pi, (lt, rt) in enumerate(((qh, kh), (ql, kh), (qh, kl))):
                            t_sc = pl.op("tensor", lambda e, lt=lt, rt=rt, bank=bank, pi=pi: e.matmul(
                                PS[bank][:, :], lt, rt, start=(pi == 0), stop=(pi == 2)),
                                inc="t" if pi == 2 else None)
                        dst = ar.f32("S_sb")[:, c * 512:(c + 1) * 512]
                        if c == nch - 1:
                            off = (q0 + s_ * 128 - (nch - 1) * 512) // 128
                            mk = ar.bf16("masks")[:, off * 512:(off + 1) * 512]
                            e_ev = pl.op("vector", lambda e, bank=bank, dst=dst, mk=mk: e.tensor_tensor(
                                out=dst, in0=PS[bank][:, :], in1=mk, op=ALU.add),
                                waits=[("t", t_sc), ("v", v_init), ("a", sg_a)], inc="v")
                            sc_ev[bank] = [("v", e_ev)]
                            wv = [("v", e_ev)]
                        else:
                            e_ev = pl.op("scalar", lambda e, bank=bank, dst=dst: e.activation(
                                out=dst, in_=PS[bank][:, :], func=AF.Copy),
                                waits=[("t", t_sc), ("v", sg_v)], inc="a")
                            sc_ev[bank] = [("a", e_ev)]
                            wv = [("a", e_ev)]
                        pl.op("vector", lambda e, dst=dst: e.tensor_reduce(
                            out=tiny(T_NGC), in_=dst, axis=AX.X, op=ALU.max, negate=True),
                            waits=wv + [("v", v_init)], flush=True)
                        v_mx = pl.op("vector", lambda e, s_=s_: e.tensor_tensor(
                            out=tiny(T_NGM + s_), in0=tiny(T_NGM + s_), in1=tiny(T_NGC), op=ALU.min),
                            inc="v", flush=True)
                    v_ac = 0
                    a_x = 0
                    a_xs = []
                    pl.op("scalar", lambda e: e.preload_act_table(AF.Exp))
                    t_pguard = pl.cnt["t"]
                    for c in range(nch):
                        rsl = T_RSC + c % 2
                        a_x = pl.op("scalar", lambda e, c=c, s_=s_, rsl=rsl: e.activation(
                            out=ar.f32("P_sb")[:, c * 512:(c + 1) * 512],
                            in_=ar.f32("S_sb")[:, c * 512:(c + 1) * 512],
                            func=AF.Exp, bias=tiny(T_NGM + s_), accum_out=tiny(rsl)),
                            waits=[("v", v_mx), ("t", t_pguard if c == 0 else 0)]
                            + ([("v", pl.cnt["v"])] if c >= 2 else []), inc="a")
                        v_ac = pl.op("vector", lambda e, s_=s_, rsl=rsl: e.tensor_tensor(
                            out=tiny(T_RSR + s_), in0=tiny(T_RSR + s_), in1=tiny(rsl), op=ALU.add),
                            waits=[("a", a_x)], inc="v", flush=True)
                        a_xs.append(a_x)
                    v_rr = pl.op("vector", lambda e, s_=s_: e.reciprocal(
                        out=tiny(T_RRC + s_), in_=tiny(T_RSR + s_)),
                        waits=[("v", v_ac), ("a", a_guard)], inc="v", flush=True)
                    for c in range(nch):
                        jmax = min(4, KC - 4 * c)
                        if jmax <= 0:
                            continue
                        ptbank = 2 + c % 2
                        pl.wait("tensor", *pt_ev[ptbank])
                        pt_ev[ptbank] = []
                        pl.wait("tensor", ("a", a_xs[c]))
                        t_pt = None
                        for j in range(jmax):
                            t_pt = pl.op("tensor", lambda e, c=c, j=j, ptbank=ptbank: e.transpose(
                                PS[ptbank][:, j * 128:(j + 1) * 128],
                                ar.f32("P_sb")[:, c * 512 + j * 128:c * 512 + (j + 1) * 128], ident),
                                inc="t" if j == jmax - 1 else None)
                        dst_hi = PT_b[:, 4 * c:4 * c + jmax, 0, s_ * 128:(s_ + 1) * 128]
                        dst_lo = PT_b[:, 4 * c:4 * c + jmax, 1, s_ * 128:(s_ + 1) * 128]
                        src3 = PS[ptbank][:, :jmax * 128].rearrange("p (j c) -> p j c", c=128)
                        a_ph = pl.op("scalar", lambda e, src3=src3, dst_hi=dst_hi: e.activation(
                            out=dst_hi, in_=src3, func=AF.Copy),
                            waits=[("t", t_pt)], inc="a")
                        v_pl = pl.op("vector", lambda e, src3=src3, dst_hi=dst_hi, dst_lo=dst_lo: e.tensor_tensor(
                            out=dst_lo, in0=src3, in1=dst_hi, op=ALU.subtract),
                            waits=[("a", a_ph)], inc="v")
                        pt_ev[ptbank] = [("a", a_ph), ("v", v_pl)]
                mm_bank_wait(4)
                pl.wait("tensor", ("v", pl.cnt["v"]), ("a", pl.cnt["a"]))
                t_pv = None
                for kc in range(KC):
                    vh = ar.bf16("Vth")[:, kc * 128:(kc + 1) * 128]
                    vl = ar.bf16("Vtl")[:, kc * 128:(kc + 1) * 128]
                    ph = ar.bf16("PT")[:, kc * 512:kc * 512 + 256]
                    plo = ar.bf16("PT")[:, kc * 512 + 256:kc * 512 + 512]
                    for pi, (lt, rt) in enumerate(((vh, ph), (vl, ph), (vh, plo))):
                        last = (kc == KC - 1 and pi == 2)
                        t_pv = pl.op("tensor", lambda e, lt=lt, rt=rt, kc=kc, pi=pi, last=last: e.matmul(
                            PS[4][:, :256], lt, rt, start=(kc == 0 and pi == 0), stop=last),
                            inc="t" if last else None)
                a_oh = pl.op("scalar", lambda e: e.activation(
                    out=ar.bf16("Oh"), in_=PS[4][:, :256], func=AF.Copy),
                    waits=[("t", t_pv)], inc="a")
                v_ol = pl.op("vector", lambda e: e.tensor_tensor(
                    out=ar.bf16("Ol"), in0=PS[4][:, :256], in1=ar.bf16("Oh"), op=ALU.subtract),
                    waits=[("a", a_oh)], inc="v")
                bank_free[4] = [("v", v_ol)]
                pl.wait("tensor", ("a", a_oh), ("v", v_ol))
                opj_free = {5: [], 0: []}
                last_b0 = None
                for s_ in (0, 1):
                    ev_list = []
                    for eg in range(4):
                        ob = 5 if eg % 2 == 0 else 0
                        pl.wait("tensor", *opj_free[ob])
                        opj_free[ob] = []
                        oh = ar.bf16("Oh")[:, s_ * 128:(s_ + 1) * 128]
                        ol = ar.bf16("Ol")[:, s_ * 128:(s_ + 1) * 128]
                        wh_ = ar.bf16("WOh")[:, eg * 512:(eg + 1) * 512]
                        wl_ = ar.bf16("WOl")[:, eg * 512:(eg + 1) * 512]
                        t_op = None
                        for pi, (lt, rt) in enumerate(((oh, wh_), (ol, wh_), (oh, wl_))):
                            t_op = pl.op("tensor", lambda e, lt=lt, rt=rt, pi=pi, ob=ob: e.matmul(
                                PS[ob][:, :], lt, rt, start=(pi == 0), stop=(pi == 2)),
                                inc="t" if pi == 2 else None)
                        dst = ar.f32("rstg")[:, eg * 512:(eg + 1) * 512]
                        ewaits = [("t", t_op)] + ([("sg", rstg_free)] if eg == 0 else [])
                        ev = pl.op("scalar", lambda e, dst=dst, s_=s_, ob=ob: e.activation(
                            out=dst, in_=PS[ob][:, :], func=AF.Copy, scale=tiny(T_RRC + s_)),
                            waits=ewaits, inc="a")
                        ev_list.append(("a", ev))
                        opj_free[ob] = [ev_list[-1]]
                        if ob == 0:
                            last_b0 = ("a", ev)
                        else:
                            bank_free[5] = [ev_list[-1]]
                    rdst = rs_in[q0 + s_ * 128:q0 + (s_ + 1) * 128, :]
                    rstg_free = pl.dma("gpsimd", rdst, ar.f32("rstg"), waits=ev_list,
                                       accum=(hidx > 0))
                if last_b0 is not None:
                    sc_ev[0] = sc_ev[0] + [last_b0]

        for hidx in range(A_HEADS):
            phase_a1(hidx)
            pl.barrier()
            if hidx == 0 and "vpre" in dbg:
                vp = dbg["vpre"].rearrange("(n p) s -> n p s", p=128)
                pl.dma("gpsimd", vp[0], ar.bf16("Vth"))
                pl.dma("gpsimd", vp[1], ar.bf16("Vtl"))
                pl.dma("gpsimd", dbg["vtf"], ar.f32("vtf"))
                if "w32d" in dbg:
                    pl.dma("gpsimd", dbg["w32d"], ar.f32("wq32")[:, 2 * 2048:3 * 2048])
                    pl.dma("gpsimd", dbg["whd"], ar.bf16("wqh")[:, 2 * 2048:3 * 2048])
                if "ps5" in dbg:
                    a_d = pl.op("scalar", lambda e: e.activation(
                        out=ar.f32("rstg")[:, 0:512], in_=PS[3][:, :], func=AF.Copy),
                        inc="a")
                    pl.dma("sync", dbg["ps5"], ar.f32("rstg")[:, 0:512],
                           waits=[("a", a_d)])
                pl.barrier()
            phase_a2(hidx, hidx)
            pl.barrier()

        if "rsin" in dbg:
            pl.dma("gpsimd", dbg["rsin"], rs_in)
            pl.barrier()
        if "ssb" in dbg:
            pl.dma("gpsimd", dbg["ssb"], ar.f32("S_sb"))
            pl.dma("gpsimd", dbg["psb"], ar.f32("P_sb"))
            pl.dma("gpsimd", dbg["pt"], ar.bf16("PT"))
            oh2 = dbg["oh"].rearrange("p (n c) -> n p c", c=256)
            pl.dma("gpsimd", oh2[0], ar.bf16("Oh"))
            pl.dma("gpsimd", oh2[1], ar.bf16("Ol"))
            pl.dma("gpsimd", dbg["rstg"], ar.f32("rstg"))
            pl.dma("gpsimd", dbg["tiny"], ar.f32("tiny"))
            pl.barrier()
        if "kvq" in dbg:
            kv = dbg["kvq"].rearrange("(n p) s -> n p s", p=128)
            for j, nm in enumerate(("KTh", "KTl", "QTh", "QTl", "Vth", "Vtl")):
                pl.dma("gpsimd", kv[j], ar.bf16(nm))
            pl.barrier()

        pl.wait("gpsimd", ("sg", pl.cnt["sg"]))
        pl.op("gpsimd", lambda e: e.collective_compute(
            "ReduceScatter", ALU.add,
            replica_groups=[[0, 1, 2, 3], [4, 5, 6, 7]],
            ins=[rs_in.opt()], outs=[rs_out.opt()]), inc="cc", real=False)

    if "F" in PHASES:
        ar.reset(g_base)
        ar.alloc("xoh", 8192)
        ar.alloc("xol", 8192)
        ar.alloc("hT", 256)
        ar.alloc("gusl", 4096)
        ar.alloc("wps", 2048)
        ar.alloc("gusl1", 4096)
        ar.alloc("wps1", 2048)
        ar.alloc("hbuf", 256)
        ar.alloc("hsil", 512)
        ar.alloc("sqs", 256)
        ar.alloc("pacc", 16384)
        ar.alias("xbufF0", "pacc", 0, D)
        ar.alias("xbufF1", "pacc", D, D)
        ar.alias("xnfF", "pacc", 2 * D, D)
        ar.alias("sqscrF", "pacc", 3 * D, D)
        ar.alias("ftrh", "pacc", 4 * D, 1024)
        ar.alias("ftrl", "pacc", 4 * D + 1024, 1024)

        xoh4 = ar.bf16("xoh").rearrange("p (dc t) -> p dc t", t=1024)
        xol4 = ar.bf16("xol").rearrange("p (dc t) -> p dc t", t=1024)

        def f_writer(i, a_hi, v_lo):
            src_h = ar.bf16("ftrh").rearrange("p (dc t) -> p dc t", t=128)
            pl.op("vector", lambda e, i=i, src_h=src_h: e.tensor_copy(
                out=xoh4[:, :, i * 128:(i + 1) * 128], in_=src_h),
                waits=[("a", a_hi)])
            src_l = ar.bf16("ftrl").rearrange("p (dc t) -> p dc t", t=128)
            v_c = pl.op("vector", lambda e, i=i, src_l=src_l: e.tensor_copy(
                out=xol4[:, :, i * 128:(i + 1) * 128], in_=src_l), inc="v")
            return ("v", v_c)

        emit_ln_tr_phase(x_own, 8, f_writer, "xnfF", "sqscrF",
                         "xbufF0", "xbufF1", ar.bf16("ftrh"), ar.bf16("ftrl"))
        pl.barrier()
        pl.wait("sync", ("cc", 1 if "Q" in PHASES else 0))

        for sb_ in range(8):
            pl.op("vector", lambda e, sb_=sb_: e.memset(
                ar.f32("pacc")[:, sb_ * 2048:(sb_ + 1) * 2048], 0.0))
            last = (sb_ == 7)
            pl.op("vector", lambda e, sb_=sb_: e.memset(tiny(T_SSQ + sb_), 0.0),
                  inc="v" if last else None, flush=True)
        v_zero = pl.cnt["v"]

        gusl4P = [ar.bf16("gusl").rearrange("p (dc w) -> p dc w", w=512),
                  ar.bf16("gusl1").rearrange("p (dc w) -> p dc w", w=512)]
        wps4P = [ar.bf16("wps").rearrange("p (hc e) -> p hc e", e=2048),
                 ar.bf16("wps1").rearrange("p (hc e) -> p hc e", e=2048)]
        agg = ag_out.rearrange("(rb r) c -> rb r c", rb=NCORE)
        gu_bank_free = {0: [], 1: []}
        slab_freeP = {0: 0, 1: 0}
        sd_prevF = 0
        pj_free = {3: [], 4: []}
        hT_free = 0
        hT4 = ar.bf16("hT").rearrange("p (hc hl c) -> p hc hl c", hc=2, hl=2)
        for hg in range(32):
            par = hg % 2
            gusl4 = gusl4P[par]
            wps4 = wps4P[par]
            for j, hcol in enumerate((256 * hg, HF + 256 * hg)):
                rb, cl = hcol // 2048, hcol % 2048
                src = agg[rb, 0:2048, cl:cl + 256].rearrange("(dc p) w -> p dc w", p=128)
                pl.dma("sync", gusl4[:, :, j * 256:(j + 1) * 256], src,
                       waits=[("t", slab_freeP[par]), ("sd", sd_prevF)] if j == 0 else ())
            for rb in range(NCORE):
                pTr = agg[rb, 2048:AGR, :].rearrange("r c -> (r c)").rearrange("(h e) -> h e", e=256)
                src = pTr[256 * hg:256 * (hg + 1), :].rearrange("(hc p) e -> p hc e", p=128)
                pl.dma("sync", wps4[:, :, rb * 256:(rb + 1) * 256], src)
            sd_sl = pl.cnt["sd"]
            sd_prevF = sd_sl
            for sub in range(8):
                gb = sub % 2
                pl.wait("tensor", *gu_bank_free[gb])
                gu_bank_free[gb] = []
                pl.wait("tensor", ("sd", sd_sl))
                t_gu = None
                first = True
                for dc in range(16):
                    for xi, xs in ((0, xoh4), (1, xol4)):
                        lt = xs[:, dc, sub * 128:(sub + 1) * 128]
                        rt = gusl4[:, dc, :]
                        last = (dc == 15) and (xi == 1)
                        t_gu = pl.op("tensor", lambda e, lt=lt, rt=rt, gb=gb, first=first, last=last: e.matmul(
                            PS[gb][:, :], lt, rt, start=first, stop=last),
                            inc="t" if last else None)
                        first = False
                pl.op("scalar", lambda e: e.preload_act_table(AF.Silu))
                a_sil = pl.op("scalar", lambda e, gb=gb: e.activation(
                    out=ar.f32("hsil"), in_=PS[gb][:, 0:512], func=AF.Silu),
                    waits=[("t", t_gu), ("v", pl.cnt["v"])], inc="a")
                v_h = pl.op("vector", lambda e, gb=gb: e.tensor_tensor(
                    out=ar.f32("hbuf"), in0=ar.f32("hsil")[:, 0:256], in1=PS[gb][:, 256:512], op=ALU.mult),
                    waits=[("a", a_sil)], inc="v")
                gu_bank_free[gb] = [("v", v_h)]
                pl.op("vector", lambda e: e.tensor_tensor(
                    out=ar.f32("sqs"), in0=ar.f32("hbuf"), in1=ar.f32("hbuf"), op=ALU.mult))
                rsl = T_SQC + sub % 2
                pl.op("vector", lambda e, rsl=rsl: e.tensor_reduce(
                    out=tiny(rsl), in_=ar.f32("sqs"), axis=AX.X, op=ALU.add), flush=True)
                pl.op("vector", lambda e, sub=sub, rsl=rsl: e.tensor_tensor(
                    out=tiny(T_SSQ + sub), in0=tiny(T_SSQ + sub), in1=tiny(rsl), op=ALU.add),
                    waits=[("v", v_zero)], flush=True)
                mm_bank_wait(2)
                pl.wait("tensor", ("v", pl.cnt["v"]), ("t", hT_free))
                t_ht = None
                for j in range(2):
                    t_ht = pl.op("tensor", lambda e, j=j: e.transpose(
                        PS[2][:, j * 128:(j + 1) * 128], ar.f32("hbuf")[:, j * 128:(j + 1) * 128], ident),
                        inc="t" if j == 1 else None)
                src2 = PS[2][:, 0:256].rearrange("p (hc c) -> p hc c", c=128)
                a_hh = pl.op("scalar", lambda e, src2=src2: e.activation(
                    out=hT4[:, :, 0, :], in_=src2, func=AF.Copy),
                    waits=[("t", t_ht)], inc="a")
                v_hl = pl.op("vector", lambda e, src2=src2: e.tensor_tensor(
                    out=hT4[:, :, 1, :], in0=src2, in1=hT4[:, :, 0, :], op=ALU.subtract),
                    waits=[("a", a_hh)], inc="v")
                bank_free[2] = [("v", v_hl)]
                for eg in range(4):
                    pjb = 3 + eg % 2
                    pl.wait("tensor", *pj_free[pjb])
                    pj_free[pjb] = []
                    pl.wait("tensor", ("a", a_hh), ("v", v_hl))
                    t_pj = None
                    first = True
                    for hc in range(2):
                        for hl in range(2):
                            lt = hT4[:, hc, hl, :]
                            rt = wps4[:, hc, eg * 512:(eg + 1) * 512]
                            last = (hc == 1) and (hl == 1)
                            t_pj = pl.op("tensor", lambda e, lt=lt, rt=rt, pjb=pjb, first=first, last=last: e.matmul(
                                PS[pjb][:, :], lt, rt, start=first, stop=last),
                                inc="t" if last else None)
                            first = False
                    dst = ar.f32("pacc")[:, sub * 2048 + eg * 512: sub * 2048 + (eg + 1) * 512]
                    v_pa = pl.op("vector", lambda e, pjb=pjb, dst=dst: e.tensor_tensor(
                        out=dst, in0=dst, in1=PS[pjb][:, :], op=ALU.add),
                        waits=[("t", t_pj)], inc="v")
                    pj_free[pjb] = [("v", v_pa)]
                hT_free = pl.cnt["t"]
            slab_freeP[par] = pl.cnt["t"]
        pl.barrier()

        G_OFF = ar.named["xoh"][0]
        gbuf = ar.h[:, G_OFF:G_OFF + 16384]
        a_g = 0
        for sub in range(8):
            pl.op("vector", lambda e, sub=sub: e.tensor_scalar(
                out=tiny(T_VAR), in0=tiny(T_SSQ + sub), scalar1=1.0 / HF,
                scalar2=RMS_EPS, op0=ALU.mult, op1=ALU.add),
                waits=[("a", pl.cnt["a"])], flush=True)
            v_rs = emit_rsqrt(T_VAR, T_FIN + sub, T_TMP)
            pl.wait("scalar", ("v", v_rs))
            pl.op("scalar", lambda e: e.preload_act_table(AF.Silu))
            a_g = pl.op("scalar", lambda e, sub=sub: e.activation(
                out=gbuf[:, sub * 2048:(sub + 1) * 2048],
                in_=ar.f32("pacc")[:, sub * 2048:(sub + 1) * 2048],
                func=AF.Silu, scale=tiny(T_FIN + sub)), inc="a")
        if "gbuf" in dbg:
            for sub in range(8):
                pl.dma("sync", dbg["gbuf"][sub * 128:(sub + 1) * 128, :],
                       gbuf[:, sub * 2048:(sub + 1) * 2048], waits=[("a", a_g)])
            pl.barrier()

    if "C" in PHASES:
        X_OFF = ar.named["pacc"][0] if "F" in PHASES else g_base
        xbC = ar.h[:, X_OFF:X_OFF + 2048]
        sbC = ar.h[:, X_OFF + 2048:X_OFF + 4096]
        G_OFF2 = ar.named["xoh"][0] if "F" in PHASES else g_base + 4096
        gbuf2 = ar.h[:, G_OFF2:G_OFF2 + 16384]
        pl.barrier()
        if "A" in PHASES:
            pl.wait("sync", ("cc", 2 if "Q" in PHASES else 1))
        if "rsout" in dbg:
            pl.dma("sync", dbg["rsout"], rs_out)
        so_fin = 0
        for sub in range(8):
            vdone = ([("v", pl.cnt["v"]), ("sd", so_fin)] if sub >= 1 else [])
            sd_xc = pl.dma("sync", xbC, x_own[sub * 128:(sub + 1) * 128, :], waits=vdone)
            sd_sc = pl.dma("sync", sbC, rs_out[sub * 128:(sub + 1) * 128, :])
            g_sl = gbuf2[:, sub * 2048:(sub + 1) * 2048]
            pl.op("vector", lambda e, g_sl=g_sl: e.tensor_tensor(
                out=sbC, in0=sbC, in1=g_sl, op=ALU.mult),
                waits=[("sd", sd_sc), ("a", pl.cnt["a"])])
            v_o = pl.op("vector", lambda e: e.tensor_tensor(
                out=sbC, in0=sbC, in1=xbC, op=ALU.add),
                waits=[("sd", sd_xc)], inc="v")
            so_fin = pl.dma("sync", out_c[sub * 128:(sub + 1) * 128, :], sbC,
                            waits=[("v", v_o)])
        pl.wait("sync", ("sd", so_fin))
        pl.wait("gpsimd", ("sg", pl.cnt["sg"]))

    pl.check()

    import contextlib
    with contextlib.ExitStack() as stack:
        SM = {}
        for s_ in ("sd", "sg", "sv", "sa", "t", "v", "a", "g", "cc", "bar"):
            SM[s_] = stack.enter_context(nc.semaphore(s_))
        block = stack.enter_context(nc.Block())
        for eng in ENGS:
            def body(e, eng=eng):
                for f in pl.ops[eng]:
                    f(e, SM)
            getattr(block, eng)(body)
    return nc


def _host_inputs(L0_attn_x, L0_attn_W_qkv, L0_attn_W_out, L0_ffn_w_gate_up, L0_ffn_w_proj):
    x = np.ascontiguousarray(np.asarray(L0_attn_x, dtype=np.float32))
    wqkv = np.asarray(L0_attn_W_qkv, dtype=np.float32)
    wout = np.asarray(L0_attn_W_out, dtype=np.float32)
    wgu = np.ascontiguousarray(np.asarray(L0_ffn_w_gate_up, dtype=np.float32))
    wp = np.ascontiguousarray(np.asarray(L0_ffn_w_proj, dtype=np.float32))
    wqkvT = np.ascontiguousarray(wqkv.T)
    woutT = np.ascontiguousarray(wout.T)
    ident = np.eye(128, dtype=np.float32)
    masks = np.zeros((128, 4 * 512), dtype=np.float32)
    for j in range(4):
        p = np.arange(128)[:, None]
        f = np.arange(512)[None, :]
        masks[:, j * 512:(j + 1) * 512] = np.where(f <= 128 * j + p, 0.0, -1.0e5)
    masks = masks.astype(ml_dtypes.bfloat16)
    in_maps = []
    for c in range(NCORE):
        seq, l = c // 4, c % 4
        h0 = 4 * l
        qcols = np.arange(h0 * 128, (h0 + 4) * 128)
        cols = np.concatenate([qcols, 2048 + qcols, 4096 + qcols])
        in_maps.append({
            "x_seq": x[seq],
            "x_own": np.ascontiguousarray(x[seq, l * OWN:(l + 1) * OWN]),
            "wqkvT": np.ascontiguousarray(wqkvT[:, cols]),
            "woutT": np.ascontiguousarray(woutT[h0 * 128:(h0 + 4) * 128, :]),
            "wgu_sh": np.ascontiguousarray(wgu[c * (GU // NCORE):(c + 1) * (GU // NCORE)]),
            "wp_sh": np.ascontiguousarray(wp[c * (D // NCORE):(c + 1) * (D // NCORE)]),
            "ident_in": ident,
            "masks_in": masks,
        })
    return in_maps


def kernel(L0_attn_x, L0_attn_W_qkv, L0_attn_W_out, L0_ffn_w_gate_up, L0_ffn_w_proj):
    in_maps = _host_inputs(L0_attn_x, L0_attn_W_qkv, L0_attn_W_out,
                           L0_ffn_w_gate_up, L0_ffn_w_proj)
    nc = build_core_program()
    res = run_bass_kernel_spmd(nc, in_maps, core_ids=list(range(NCORE)))
    out = np.empty((2, S, D), dtype=np.float32)
    for c in range(NCORE):
        seq, l = c // 4, c % 4
        out[seq, l * OWN:(l + 1) * OWN] = np.asarray(res.results[c]["out_c"])
    return out


# version 05
# speedup vs baseline: 1.0241x; 1.0186x over previous
import math
import numpy as np
import ml_dtypes
import concourse.bass as bass
import concourse.mybir as mybir
from concourse.bass_utils import run_bass_kernel_spmd

F32 = mybir.dt.float32
BF16 = mybir.dt.bfloat16
ALU = mybir.AluOpType
AF = mybir.ActivationFunctionType
AX = mybir.AxisListType

D = 2048
S = 4096
NCORE = 8
NHC = 4
DK = 128
HF = 8192
GU = 2 * HF
OWN = 1024
QTS = 256
INVSQ = float(1.0 / math.sqrt(128.0))
LN_EPS = 1e-5
RMS_EPS = float(np.float32(1.1920929e-7))
MAG = 192.0
KCMAX = 34
AGR = 3072

ENGS = ("sync", "tensor", "vector", "scalar", "gpsimd")
OWN_SEM = {"tensor": "t", "vector": "v", "scalar": "a", "gpsimd": "g"}

PHASES = ("P", "Q", "A", "F", "C")
DBG_DUMPS = ()
P_TILES = 32
A_HEADS = NHC if 'NHC' in dir() else 4


class Planner:
    def __init__(self):
        self.ops = {e: [] for e in ENGS}
        self.cnt = {}
        self.maxwait = {}
        self.uninc = {e: 0 for e in ENGS}

    def sem(self, *names):
        for n in names:
            self.cnt[n] = 0
            self.maxwait[n] = 0

    def op(self, eng, fn, waits=(), inc=None, amt=1, real=True, flush=False):
        if flush:
            own = OWN_SEM[eng]
            waits = tuple(waits) + ((own, self.cnt[own]),)
            inc = inc or own
        w = tuple((s, v) for (s, v) in waits if v > 0)
        for s, v in w:
            assert s in self.cnt, s
            self.maxwait[s] = max(self.maxwait[s], v)
        if inc is not None:
            self.cnt[inc] += amt
        if eng in OWN_SEM:
            if real:
                self.uninc[eng] += 1
                if inc == OWN_SEM[eng]:
                    self.uninc[eng] = 0

        def thunk(e, SM, fn=fn, w=w, inc=inc, amt=amt):
            for s, v in w:
                e.wait_ge(SM[s], v)
            r = fn(e)
            if inc is not None:
                r.then_inc(SM[inc], amt)
        self.ops[eng].append(thunk)
        return self.cnt[inc] if inc is not None else None

    def wait(self, eng, *pairs):
        w = tuple((s, v) for (s, v) in pairs if v > 0)
        if not w:
            return
        for s, v in w:
            self.maxwait[s] = max(self.maxwait[s], v)

        def thunk(e, SM, w=w):
            for s, v in w:
                e.wait_ge(SM[s], v)
        self.ops[eng].append(thunk)

    def dma(self, queue, out, in_, waits=(), accum=False):
        sem = {"sync": "sd", "gpsimd": "sg", "vector": "sv", "scalar": "sa"}[queue]
        if accum:
            return self.op(queue, lambda e: e.dma_start(out=out, in_=in_, accum_op=ALU.add),
                           waits=waits, inc=sem, amt=16, real=False)
        return self.op(queue, lambda e: e.dma_start(out=out, in_=in_),
                       waits=waits, inc=sem, amt=16, real=False)

    def barrier(self):
        for e in ("tensor", "vector", "scalar", "gpsimd"):
            assert self.uninc[e] == 0, f"barrier: engine {e} has {self.uninc[e]} un-inc'd ops"
        drains = (("sd", self.cnt["sd"]), ("sg", self.cnt["sg"]),
                  ("sv", self.cnt["sv"]), ("sa", self.cnt["sa"]))
        base = self.cnt["bar"]
        for e in ENGS:
            own = (((OWN_SEM[e], self.cnt[OWN_SEM[e]]),) if e in OWN_SEM else ())
            self.op(e, lambda en: en.nop(), waits=drains + own, inc="bar", real=False)
        for e in ENGS:
            self.wait(e, ("bar", base + len(ENGS)))

    def check(self):
        for s, v in self.maxwait.items():
            assert v <= self.cnt[s], f"sem {s}: wait {v} > final {self.cnt[s]}"


class Arena:
    NCOLS = 50176

    def __init__(self, handle):
        self.ptr = 0
        self.named = {}
        self.h = handle

    def alloc(self, name, cols):
        assert self.ptr + cols <= self.NCOLS, f"{name}: OOM {self.ptr}+{cols}"
        self.named[name] = (self.ptr, cols)
        self.ptr += cols

    def alias(self, name, base_name, rel_off, cols):
        o, n = self.named[base_name]
        assert rel_off + cols <= n, f"alias {name} exceeds {base_name}"
        self.named[name] = (o + rel_off, cols)

    def mark(self):
        return self.ptr

    def reset(self, p):
        self.ptr = p

    def f32(self, name):
        o, n = self.named[name]
        return self.h[:, o:o + n]

    def bf16(self, name):
        o, n = self.named[name]
        return self.h[:, o:o + n].bitcast(BF16)


def build_core_program():
    nc = bass.Bass("TRN2", target_bir_lowering=False, detect_race_conditions=False)
    pl = Planner()
    pl.sem("sd", "sg", "sv", "sa", "t", "v", "a", "g", "cc", "bar")

    x_seq = nc.dram_tensor("x_seq", [S, D], F32, kind="ExternalInput").ap()
    wqkvT = nc.dram_tensor("wqkvT", [D, 3 * NHC * DK], F32, kind="ExternalInput").ap()
    woutT = nc.dram_tensor("woutT", [NHC * DK, D], F32, kind="ExternalInput").ap()
    wgu_sh = nc.dram_tensor("wgu_sh", [GU // NCORE, D], F32, kind="ExternalInput").ap()
    wp_sh = nc.dram_tensor("wp_sh", [D // NCORE, HF], F32, kind="ExternalInput").ap()
    ident_in = nc.dram_tensor("ident_in", [128, 128], F32, kind="ExternalInput").ap()
    masks_in = nc.dram_tensor("masks_in", [128, 4 * 512], BF16, kind="ExternalInput").ap()
    x_own = nc.dram_tensor("x_own", [OWN, D], F32, kind="ExternalInput").ap()
    out_c = nc.dram_tensor("out_c", [OWN, D], F32, kind="ExternalOutput").ap()

    xsp_h = nc.dram_tensor("xsp_h", [D, S], BF16, kind="Internal").ap()
    xsp_l = nc.dram_tensor("xsp_l", [D, S], BF16, kind="Internal").ap()
    rs_in = nc.dram_tensor("rs_in", [S, D], F32, kind="Internal").ap()
    rs_out = nc.dram_tensor("rs_out", [OWN, D], F32, kind="Internal").ap()
    ag_in = nc.dram_tensor("ag_in", [AGR, D], BF16, kind="Internal").ap()
    ag_out = nc.dram_tensor("ag_out", [NCORE * AGR, D], BF16, kind="Internal",
                            addr_space="Shared").ap()

    dbg = {}
    for name in DBG_DUMPS:
        shapes = {"xsp": [D, S], "ag": [NCORE * AGR, D], "kvq": [6 * 128, S],
                  "rsin": [S, D], "rsout": [OWN, D], "gbuf": [OWN, D],
                  "xnf": [128, D], "tiny": [128, 64],
                  "ssb": [128, S], "psb": [128, S], "pt": [128, KCMAX * 512],
                  "oh": [128, 512], "rstg": [128, D],
                  "vtf": [128, 256], "vpre": [2 * 128, S], "ps5": [128, 512],
                  "w32d": [128, 2048], "whd": [128, 2048]}
        dt = {"xsp": BF16, "ag": BF16, "kvq": BF16, "rsin": F32, "rsout": F32,
              "gbuf": F32, "xnf": F32, "tiny": F32,
              "ssb": F32, "psb": F32, "pt": BF16, "oh": BF16, "rstg": F32,
              "vtf": F32, "vpre": BF16, "ps5": F32, "w32d": F32, "whd": BF16}[name]
        dbg[name] = nc.dram_tensor("dbg_" + name, shapes[name], dt, kind="ExternalOutput").ap()

    arena_h = nc.alloc_sbuf_tensor("arena", [128, Arena.NCOLS], F32)
    ar = Arena(arena_h.ap())
    PS = [nc.alloc_psum_tensor(f"ps{i}", [128, 512], F32).ap() for i in range(8)]

    ar.alloc("ident", 128)
    ar.alloc("masks", 1024)
    ar.alloc("tiny", 64)
    T_SUMX, T_SUMSQ, T_MEAN, T_VAR, T_INV, T_NMI, T_TMP = 0, 1, 2, 3, 4, 5, 6
    T_NGM, T_NGC, T_RSR, T_RSC, T_RRC = 8, 10, 12, 14, 16
    T_SSQ, T_SQC, T_FIN = 24, 32, 40

    def tiny(slot):
        o, _ = ar.named["tiny"]
        return ar.h[:, o + slot:o + slot + 1]

    g_base = ar.mark()
    ident = ar.f32("ident")

    pl.dma("sync", ident, ident_in)
    sd_const = pl.dma("sync", ar.bf16("masks"), masks_in)

    bank_free = {i: [] for i in range(8)}

    def mm_bank_wait(bank):
        pl.wait("tensor", *bank_free[bank])
        bank_free[bank] = []

    def emit_rsqrt(x_slot, out_slot, s_slot, iters=7):
        pl.op("vector", lambda e: e.tensor_scalar(
            out=tiny(s_slot), in0=tiny(x_slot), scalar1=0.5, scalar2=0.5,
            op0=ALU.mult, op1=ALU.add), flush=True)
        for _ in range(iters):
            pl.op("vector", lambda e: e.reciprocal(out=tiny(out_slot), in_=tiny(s_slot)), flush=True)
            pl.op("vector", lambda e: e.tensor_tensor(
                out=tiny(out_slot), in0=tiny(x_slot), in1=tiny(out_slot), op=ALU.mult), flush=True)
            pl.op("vector", lambda e: e.tensor_tensor(
                out=tiny(s_slot), in0=tiny(s_slot), in1=tiny(out_slot), op=ALU.add), flush=True)
            pl.op("vector", lambda e: e.tensor_scalar_mul(
                out=tiny(s_slot), in0=tiny(s_slot), scalar1=0.5), flush=True)
        return pl.op("vector", lambda e: e.reciprocal(
            out=tiny(out_slot), in_=tiny(s_slot)), flush=True)

    def emit_ln(xb, xnf, sqscr, sd_x, t_guard):
        v_guard = pl.cnt["v"]
        pl.op("vector", lambda e: e.tensor_reduce(
            out=tiny(T_SUMX), in_=xb, axis=AX.X, op=ALU.add), waits=[("sd", sd_x)],
            flush=True)
        pl.op("scalar", lambda e: e.preload_act_table(AF.Square))
        a_sq = pl.op("scalar", lambda e: e.activation(
            out=sqscr, in_=xb, func=AF.Square, accum_out=tiny(T_SUMSQ)),
            waits=[("sd", sd_x), ("v", v_guard)], inc="a")
        pl.op("vector", lambda e: e.tensor_scalar_mul(out=tiny(T_MEAN), in0=tiny(T_SUMX), scalar1=1.0 / D),
              flush=True)
        pl.op("vector", lambda e: e.tensor_scalar_mul(out=tiny(T_VAR), in0=tiny(T_SUMSQ), scalar1=1.0 / D),
              waits=[("a", a_sq)], flush=True)
        pl.op("vector", lambda e: e.tensor_tensor(out=tiny(T_TMP), in0=tiny(T_MEAN), in1=tiny(T_MEAN), op=ALU.mult),
              flush=True)
        pl.op("vector", lambda e: e.tensor_tensor(out=tiny(T_VAR), in0=tiny(T_VAR), in1=tiny(T_TMP), op=ALU.subtract),
              flush=True)
        pl.op("vector", lambda e: e.tensor_scalar_add(out=tiny(T_VAR), in0=tiny(T_VAR), scalar1=LN_EPS),
              flush=True)
        emit_rsqrt(T_VAR, T_INV, T_TMP, iters=4)
        pl.op("vector", lambda e: e.tensor_scalar_mul(out=tiny(T_NMI), in0=tiny(T_MEAN), scalar1=-1.0),
              flush=True)
        v_xnf = pl.op("vector", lambda e: e.tensor_scalar(
            out=xnf, in0=xb, scalar1=tiny(T_NMI), scalar2=tiny(T_INV),
            op0=ALU.add, op1=ALU.mult),
            waits=[("t", t_guard)], inc="v", flush=True)
        return a_sq, v_xnf

    def emit_ln_tr_phase(x_src, n_tiles, writer, xnf_name, sq_name,
                         xb0_name, xb1_name, hi_stg, lo_stg):
        done_xb = [(0, 0), (0, 0)]
        wr_guard = (None, 0)
        for i in range(n_tiles):
            bsel = i % 2
            xb = ar.f32(xb0_name) if bsel == 0 else ar.f32(xb1_name)
            sd_x = pl.dma("sync", xb, x_src[i * 128:(i + 1) * 128, :],
                          waits=[("a", done_xb[bsel][0]), ("v", done_xb[bsel][1])])
            a_sq, v_xnf = emit_ln(xb, ar.f32(xnf_name), ar.f32(sq_name), sd_x, pl.cnt["t"])
            done_xb[bsel] = (a_sq, v_xnf)
            pb = 0 if bsel == 0 else 4
            bw = []
            for b in range(4):
                bw += bank_free[pb + b]
                bank_free[pb + b] = []
            pl.wait("tensor", ("v", v_xnf), ("sd", sd_const), *bw)
            t_tr = None
            for dc in range(16):
                bank, col = pb + dc // 4, (dc % 4) * 128
                t_tr = pl.op("tensor", lambda e, dc=dc, bank=bank, col=col: e.transpose(
                    PS[bank][:, col:col + 128],
                    ar.f32(xnf_name)[:, dc * 128:(dc + 1) * 128], ident),
                    inc="t" if dc == 15 else None)
            a_hi = v_lo = 0
            for b in range(4):
                ew = [("t", t_tr)]
                if b == 0 and wr_guard[0] is not None:
                    ew.append(wr_guard)
                a_hi = pl.op("scalar", lambda e, pb=pb, b=b: e.activation(
                    out=hi_stg[:, b * 512:(b + 1) * 512], in_=PS[pb + b][:, :], func=AF.Copy),
                    waits=ew, inc="a")
                v_lo = pl.op("vector", lambda e, pb=pb, b=b: e.tensor_tensor(
                    out=lo_stg[:, b * 512:(b + 1) * 512], in0=PS[pb + b][:, :],
                    in1=hi_stg[:, b * 512:(b + 1) * 512], op=ALU.subtract),
                    waits=[("a", a_hi)], inc="v")
                bank_free[pb + b] = [("v", v_lo)]
            wr_guard = writer(i, a_hi, v_lo)

    if "P" in PHASES:
        ar.reset(g_base)
        ar.alloc("xbuf0", D)
        ar.alloc("xbuf1", D)
        ar.alloc("sqscr", D)
        ar.alloc("xnf", D)
        ar.alloc("trsh", 1024)
        ar.alloc("trsl", 1024)

        def p_writer(i, a_hi, v_lo):
            src_h = ar.bf16("trsh").rearrange("p (dc t) -> p dc t", t=128)
            dst_h = xsp_h.rearrange("(dc p) (i t) -> p dc i t", p=128, t=128)[:, :, i, :]
            pl.dma("sync", dst_h, src_h, waits=[("a", a_hi), ("v", v_lo)])
            src_l = ar.bf16("trsl").rearrange("p (dc t) -> p dc t", t=128)
            dst_l = xsp_l.rearrange("(dc p) (i t) -> p dc i t", p=128, t=128)[:, :, i, :]
            sd_w = pl.dma("sync", dst_l, src_l)
            return ("sd", sd_w)

        emit_ln_tr_phase(x_seq, P_TILES, p_writer, "xnf", "sqscr",
                         "xbuf0", "xbuf1", ar.bf16("trsh"), ar.bf16("trsl"))
        pl.barrier()
        if "xnf" in dbg:
            pl.dma("gpsimd", dbg["xnf"], ar.f32("xnf"))
        if "tiny" in dbg:
            pl.dma("gpsimd", dbg["tiny"], ar.f32("tiny"))
        if "xsp" in dbg:
            pl.dma("gpsimd", dbg["xsp"], xsp_h)
        if dbg:
            pl.barrier()

    if "Q" in PHASES:
        ar.reset(g_base)
        ar.alloc("qwf", HF)
        ar.alloc("qwb", HF // 2)
        ar.alloc("qratio", HF // 2)
        ar.alloc("qsc", 128)
        ar.alloc("qscb", 64)
        ar.alloc("qrec", 128)
        ar.alloc("qst", 512)
        qst_free = 0
        tile_free = 0

        def quant_tile(src_ap, w, ngrp, dst_writer):
            nonlocal qst_free, tile_free
            sg_b = pl.dma("gpsimd", ar.bf16("qwb")[:, :w], src_ap, waits=[("t", tile_free)])
            qwb_g = ar.bf16("qwb")[:, :w].rearrange("p (g k) -> p g k", k=64)
            pl.op("vector", lambda e: e.tensor_reduce(
                out=ar.f32("qsc")[:, :ngrp], in_=qwb_g, axis=AX.X, op=ALU.add,
                apply_absolute_value=True), waits=[("sg", sg_b), ("t", tile_free)],
                flush=True)
            pl.op("vector", lambda e: e.tensor_scalar(
                out=ar.bf16("qscb")[:, :ngrp], in0=ar.f32("qsc")[:, :ngrp],
                scalar1=1.0 / 64.0, scalar2=1e-8, op0=ALU.mult, op1=ALU.max),
                flush=True)
            pl.op("vector", lambda e: e.tensor_copy(
                out=ar.f32("qrec")[:, :ngrp], in_=ar.bf16("qscb")[:, :ngrp]),
                flush=True)
            pl.op("vector", lambda e: e.reciprocal(
                out=ar.f32("qrec")[:, :ngrp], in_=ar.f32("qrec")[:, :ngrp]),
                flush=True)
            recb = ar.f32("qrec")[:, :ngrp].unsqueeze(2).broadcast_to((128, ngrp, 64))
            sclb = ar.bf16("qscb")[:, :ngrp].unsqueeze(2).broadcast_to((128, ngrp, 64))
            ratio_b = ar.bf16("qratio")[:, :w]
            ratio_g = ratio_b.rearrange("p (g k) -> p g k", k=64)
            pl.op("vector", lambda e: e.tensor_tensor(out=ratio_g, in0=qwb_g, in1=recb, op=ALU.mult),
                flush=True)
            pl.op("vector", lambda e: e.tensor_scalar_add(out=ratio_b, in0=ratio_b, scalar1=MAG))
            pl.op("vector", lambda e: e.tensor_scalar(
                out=ratio_b, in0=ratio_b, scalar1=MAG, scalar2=1.0,
                op0=ALU.subtract, op1=ALU.min))
            v3 = pl.op("vector", lambda e: e.scalar_tensor_tensor(
                out=ar.f32("qwf")[:, :w].rearrange("p (g k) -> p g k", k=64),
                in0=ratio_g, scalar=-1.0, in1=sclb,
                op0=ALU.max, op1=ALU.mult), inc="v")
            nblk = w // 128
            for b0 in range(0, nblk, 8):
                nb = min(8, nblk - b0)
                pl.wait("tensor", ("v", v3), ("sd", qst_free),
                        *(bank_free[0] + bank_free[1]))
                bank_free[0] = []
                bank_free[1] = []
                t_q = None
                for j in range(nb):
                    bank, col = j // 4, (j % 4) * 128
                    t_q = pl.op("tensor", lambda e, j=j, b0=b0, bank=bank, col=col: e.transpose(
                        PS[bank][:, col:col + 128],
                        ar.f32("qwf")[:, (b0 + j) * 128:(b0 + j + 1) * 128], ident),
                        inc="t" if j == nb - 1 else None)
                a_e = pl.op("scalar", lambda e: e.activation(
                    out=ar.bf16("qst")[:, 0:512], in_=PS[0][:, :], func=AF.Copy),
                    waits=[("t", t_q)], inc="a")
                if nb > 4:
                    a_e = pl.op("scalar", lambda e: e.activation(
                        out=ar.bf16("qst")[:, 512:1024], in_=PS[1][:, :], func=AF.Copy),
                        inc="a")
                bank_free[0] = [("a", a_e)]
                bank_free[1] = [("a", a_e)]
                qst_free = dst_writer(b0, nb, a_e)
            tile_free = pl.cnt["t"]

        for ht in range(16):
            def wr_gu(b0, nb, a_e, ht=ht):
                src = ar.bf16("qst")[:, :nb * 128].rearrange("p (j c) -> p j c", c=128)
                dst = ag_in.rearrange("(dc p) (hc c) -> p dc hc c", p=128, c=128)[:, b0:b0 + nb, ht, :]
                return pl.dma("sync", dst, src, waits=[("a", a_e)])
            quant_tile(wgu_sh[ht * 128:(ht + 1) * 128, :], D, 32, wr_gu)
        pT = ag_in[2048:AGR, :].rearrange("r c -> (r c)").rearrange("(h e) -> h e", e=256)
        for et in range(2):
            def wr_p(b0, nb, a_e, et=et):
                src = ar.bf16("qst")[:, :nb * 128].rearrange("p (j c) -> p j c", c=128)
                dst = pT.rearrange("(j p) e -> p j e", p=128)[:, b0:b0 + nb, et * 128:(et + 1) * 128]
                return pl.dma("sync", dst, src, waits=[("a", a_e)])
            quant_tile(wp_sh[et * 128:(et + 1) * 128, :], HF, 128, wr_p)

        pl.barrier()
        pl.op("gpsimd", lambda e: e.collective_compute(
            "AllGather", ALU.bypass, replica_groups=[list(range(NCORE))],
            ins=[ag_in.opt()], outs=[ag_out.opt()]), inc="cc", real=False)
        if "ag" in dbg:
            pl.wait("gpsimd", ("cc", 1))
            pl.dma("gpsimd", dbg["ag"], ag_out)
            pl.barrier()

    if "A" in PHASES:
        ar.reset(g_base)
        ar.alloc("KTh", D)
        ar.alloc("KTl", D)
        ar.alloc("QTh", D)
        ar.alloc("QTl", D)
        ar.alloc("Vth", D)
        ar.alloc("Vtl", D)
        ar.alloc("WOh", D // 2)
        ar.alloc("WOl", D // 2)
        ar.alloc("rstg", D)
        ar.alloc("vtf", 256)
        a_scr = ar.mark()
        ar.alloc("wq32", 16 * 384)
        ar.alloc("wqh", 16 * 384 // 2)
        ar.alloc("wql", 16 * 384 // 2)
        ar.alloc("xsl_h0", 2048)
        ar.alloc("xsl_l0", 2048)
        ar.alloc("xsl_h1", 2048)
        ar.alloc("xsl_l1", 2048)
        a2_base = a_scr

        def phase_a1(h):
            wo_src = woutT[h * 128:(h + 1) * 128, :]
            sd_wo = pl.dma("sync", ar.f32("wq32")[:, :D], wo_src)
            sg_wo = pl.dma("gpsimd", ar.bf16("WOh"), wo_src)
            v_wo = pl.op("vector", lambda e: e.tensor_tensor(
                out=ar.bf16("WOl"), in0=ar.f32("wq32")[:, :D], in1=ar.bf16("WOh"), op=ALU.subtract),
                waits=[("sd", sd_wo), ("sg", sg_wo)], inc="v")
            w32 = ar.f32("wq32").rearrange("p (j dc c) -> p j dc c", j=3, c=128)
            wh = ar.bf16("wqh").rearrange("p (j dc c) -> p j dc c", j=3, c=128)
            wl = ar.bf16("wql").rearrange("p (j dc c) -> p j dc c", j=3, c=128)
            cols = [h * 128, NHC * 128 + h * 128, 2 * NHC * 128 + h * 128]
            sd_w = 0
            for j, c in enumerate(cols):
                src = wqkvT[:, c:c + 128].rearrange("(dc p) e -> p dc e", p=128)
                sd_w = pl.dma("sync", w32[:, j], src,
                              waits=[("v", v_wo)] if j == 0 else ())
            whf, wlf = ar.bf16("wqh"), ar.bf16("wql")
            w32f = ar.f32("wq32")
            pl.op("scalar", lambda e: e.activation(
                out=whf[:, 0:2048], in_=w32f[:, 0:2048], func=AF.Copy, scale=INVSQ),
                waits=[("sd", sd_w)])
            pl.op("scalar", lambda e: e.activation(
                out=whf[:, 2048:4096], in_=w32f[:, 2048:4096], func=AF.Copy))
            a_wh = pl.op("scalar", lambda e: e.activation(
                out=whf[:, 4096:6144], in_=w32f[:, 4096:6144], func=AF.Copy), inc="a")
            pl.op("vector", lambda e: e.scalar_tensor_tensor(
                out=wlf[:, 0:2048], in0=w32f[:, 0:2048], scalar=INVSQ,
                in1=whf[:, 0:2048], op0=ALU.mult, op1=ALU.subtract),
                waits=[("a", a_wh)])
            v_wl = pl.op("vector", lambda e: e.tensor_tensor(
                out=wlf[:, 2048:6144], in0=w32f[:, 2048:6144], in1=whf[:, 2048:6144],
                op=ALU.subtract), inc="v")
            sd_prev = sd_w
            for sp in range(16):
                bsel = sp % 2
                xh = ar.bf16("xsl_h0") if bsel == 0 else ar.bf16("xsl_h1")
                xl = ar.bf16("xsl_l0") if bsel == 0 else ar.bf16("xsl_l1")
                dwaits = [("sd", sd_prev)] + ([("t", pl.cnt["t"])] if sp >= 2 else [])
                xh3 = xh.rearrange("p (dc s) -> p dc s", s=256)
                xl3 = xl.rearrange("p (dc s) -> p dc s", s=256)
                pl.dma("sync", xh3, xsp_h.rearrange("(dc p) s -> p dc s", p=128)[:, :, sp * 256:(sp + 1) * 256],
                       waits=dwaits)
                sd_sl = pl.dma("sync", xl3, xsp_l.rearrange("(dc p) s -> p dc s", p=128)[:, :, sp * 256:(sp + 1) * 256])
                sd_prev = sd_sl
                pb = 0 if bsel == 0 else 3
                pl.wait("tensor", ("sd", sd_sl), ("v", v_wl))
                for bank in (pb, pb + 1, pb + 2):
                    mm_bank_wait(bank)
                t_p = None
                for gi, (proj, bank) in enumerate(((2, pb + 0), (1, pb + 1), (0, pb + 2))):
                    first = True
                    for dc in range(16):
                        whc = wh[:, proj, dc, :]
                        wlc = wl[:, proj, dc, :]
                        xhc = xh3[:, dc, :]
                        xlc = xl3[:, dc, :]
                        pairs = ((whc, xhc), (wlc, xhc), (whc, xlc))
                        for pi, (lt, rt) in enumerate(pairs):
                            last_all = (gi == 2) and (dc == 15) and (pi == 2)
                            t_p = pl.op("tensor", lambda e, lt=lt, rt=rt, bank=bank,
                                        first=first, last=(dc == 15 and pi == 2): e.matmul(
                                PS[bank][:, :256], lt, rt, start=first, stop=last),
                                inc="t" if last_all else None)
                            first = False
                co = sp * 256
                v_vf = pl.op("vector", lambda e, pb=pb: e.tensor_copy(
                    out=ar.f32("vtf"), in_=PS[pb][:, :256]),
                    waits=[("t", t_p)], inc="v")
                bank_free[pb] = [("v", v_vf)]
                a_k = pl.op("scalar", lambda e, pb=pb, co=co: e.activation(
                    out=ar.bf16("KTh")[:, co:co + 256], in_=PS[pb + 1][:, :256], func=AF.Copy),
                    waits=[("t", t_p)], inc="a")
                v_k = pl.op("vector", lambda e, pb=pb, co=co: e.tensor_tensor(
                    out=ar.bf16("KTl")[:, co:co + 256], in0=PS[pb + 1][:, :256],
                    in1=ar.bf16("KTh")[:, co:co + 256], op=ALU.subtract),
                    waits=[("a", a_k)], inc="v")
                bank_free[pb + 1] = [("v", v_k)]
                a_q = pl.op("scalar", lambda e, pb=pb, co=co: e.activation(
                    out=ar.bf16("QTh")[:, co:co + 256], in_=PS[pb + 2][:, :256], func=AF.Copy),
                    inc="a")
                v_q = pl.op("vector", lambda e, pb=pb, co=co: e.tensor_tensor(
                    out=ar.bf16("QTl")[:, co:co + 256], in0=PS[pb + 2][:, :256],
                    in1=ar.bf16("QTh")[:, co:co + 256], op=ALU.subtract),
                    waits=[("a", a_q)], inc="v")
                bank_free[pb + 2] = [("v", v_q)]
                mm_bank_wait(6)
                pl.wait("tensor", ("v", v_vf))
                t_vt = None
                for j in range(2):
                    t_vt = pl.op("tensor", lambda e, j=j: e.transpose(
                        PS[6][:, j * 128:(j + 1) * 128], ar.f32("vtf")[:, j * 128:(j + 1) * 128], ident),
                        inc="t" if j == 1 else None)
                a_vh = pl.op("scalar", lambda e, co=co: e.activation(
                    out=ar.bf16("Vth")[:, co:co + 256], in_=PS[6][:, 0:256], func=AF.Copy),
                    waits=[("t", t_vt)], inc="a")
                v_vl = pl.op("vector", lambda e, co=co: e.tensor_tensor(
                    out=ar.bf16("Vtl")[:, co:co + 256], in0=PS[6][:, 0:256],
                    in1=ar.bf16("Vth")[:, co:co + 256], op=ALU.subtract),
                    waits=[("a", a_vh)], inc="v")
                bank_free[6] = [("v", v_vl)]

        def phase_a2(h, hidx):
            ar.reset(a2_base)
            ar.alloc("S_sb", S)
            ar.alloc("P_sb", S)
            ar.alloc("PT", KCMAX * 256)
            ar.alloc("Oh", 128)
            ar.alloc("Ol", 128)
            sc_ev = {0: [], 1: []}
            pt_ev = {2: [], 3: []}
            rstg_free = 0
            PT_b = ar.bf16("PT").rearrange("p (kc hl sq) -> p kc hl sq", hl=2, sq=256)
            for qi in range(16):
                q0 = QTS * qi
                nch = q0 // 512 + 1
                KC = 2 * qi + 2
                a_guard = pl.cnt["a"]
                pl.op("vector", lambda e: e.memset(tiny(T_NGM + 0), 3e38),
                      waits=[("a", a_guard)], flush=True)
                pl.op("vector", lambda e: e.memset(tiny(T_NGM + 1), 3e38), flush=True)
                pl.op("vector", lambda e: e.memset(tiny(T_RSR + 0), 0.0), flush=True)
                v_init = pl.op("vector", lambda e: e.memset(tiny(T_RSR + 1), 0.0), inc="v",
                               flush=True)
                for s_ in (0, 1):
                    sg_a = pl.cnt["a"]
                    sg_v = pl.cnt["v"]
                    for c in range(nch):
                        bank = c % 2
                        pl.wait("tensor", *sc_ev[bank])
                        sc_ev[bank] = []
                        qh = ar.bf16("QTh")[:, q0 + s_ * 128:q0 + (s_ + 1) * 128]
                        ql = ar.bf16("QTl")[:, q0 + s_ * 128:q0 + (s_ + 1) * 128]
                        kh = ar.bf16("KTh")[:, c * 512:(c + 1) * 512]
                        kl = ar.bf16("KTl")[:, c * 512:(c + 1) * 512]
                        t_sc = None
                        for pi, (lt, rt) in enumerate(((qh, kh), (ql, kh), (qh, kl))):
                            t_sc = pl.op("tensor", lambda e, lt=lt, rt=rt, bank=bank, pi=pi: e.matmul(
                                PS[bank][:, :], lt, rt, start=(pi == 0), stop=(pi == 2)),
                                inc="t" if pi == 2 else None)
                        dst = ar.f32("S_sb")[:, c * 512:(c + 1) * 512]
                        if c == nch - 1:
                            off = (q0 + s_ * 128 - (nch - 1) * 512) // 128
                            mk = ar.bf16("masks")[:, off * 512:(off + 1) * 512]
                            e_ev = pl.op("vector", lambda e, bank=bank, dst=dst, mk=mk: e.tensor_tensor(
                                out=dst, in0=PS[bank][:, :], in1=mk, op=ALU.add),
                                waits=[("t", t_sc), ("v", v_init), ("a", sg_a)], inc="v")
                            sc_ev[bank] = [("v", e_ev)]
                            wv = [("v", e_ev)]
                        else:
                            e_ev = pl.op("scalar", lambda e, bank=bank, dst=dst: e.activation(
                                out=dst, in_=PS[bank][:, :], func=AF.Copy),
                                waits=[("t", t_sc), ("v", sg_v)], inc="a")
                            sc_ev[bank] = [("a", e_ev)]
                            wv = [("a", e_ev)]
                        pl.op("vector", lambda e, dst=dst: e.tensor_reduce(
                            out=tiny(T_NGC), in_=dst, axis=AX.X, op=ALU.max, negate=True),
                            waits=wv + [("v", v_init)], flush=True)
                        v_mx = pl.op("vector", lambda e, s_=s_: e.tensor_tensor(
                            out=tiny(T_NGM + s_), in0=tiny(T_NGM + s_), in1=tiny(T_NGC), op=ALU.min),
                            inc="v", flush=True)
                    v_ac = 0
                    a_x = 0
                    a_xs = []
                    pl.op("scalar", lambda e: e.preload_act_table(AF.Exp))
                    t_pguard = pl.cnt["t"]
                    for c in range(nch):
                        rsl = T_RSC + c % 2
                        a_x = pl.op("scalar", lambda e, c=c, s_=s_, rsl=rsl: e.activation(
                            out=ar.f32("P_sb")[:, c * 512:(c + 1) * 512],
                            in_=ar.f32("S_sb")[:, c * 512:(c + 1) * 512],
                            func=AF.Exp, bias=tiny(T_NGM + s_), accum_out=tiny(rsl)),
                            waits=[("v", v_mx), ("t", t_pguard if c == 0 else 0)]
                            + ([("v", pl.cnt["v"])] if c >= 2 else []), inc="a")
                        v_ac = pl.op("vector", lambda e, s_=s_, rsl=rsl: e.tensor_tensor(
                            out=tiny(T_RSR + s_), in0=tiny(T_RSR + s_), in1=tiny(rsl), op=ALU.add),
                            waits=[("a", a_x)], inc="v", flush=True)
                        a_xs.append(a_x)
                    v_rr = pl.op("vector", lambda e, s_=s_: e.reciprocal(
                        out=tiny(T_RRC + s_), in_=tiny(T_RSR + s_)),
                        waits=[("v", v_ac), ("a", a_guard)], inc="v", flush=True)
                    for c in range(nch):
                        jmax = min(4, KC - 4 * c)
                        if jmax <= 0:
                            continue
                        ptbank = 2 + c % 2
                        pl.wait("tensor", *pt_ev[ptbank])
                        pt_ev[ptbank] = []
                        pl.wait("tensor", ("a", a_xs[c]))
                        t_pt = None
                        for j in range(jmax):
                            t_pt = pl.op("tensor", lambda e, c=c, j=j, ptbank=ptbank: e.transpose(
                                PS[ptbank][:, j * 128:(j + 1) * 128],
                                ar.f32("P_sb")[:, c * 512 + j * 128:c * 512 + (j + 1) * 128], ident),
                                inc="t" if j == jmax - 1 else None)
                        dst_hi = PT_b[:, 4 * c:4 * c + jmax, 0, s_ * 128:(s_ + 1) * 128]
                        dst_lo = PT_b[:, 4 * c:4 * c + jmax, 1, s_ * 128:(s_ + 1) * 128]
                        src3 = PS[ptbank][:, :jmax * 128].rearrange("p (j c) -> p j c", c=128)
                        a_ph = pl.op("scalar", lambda e, src3=src3, dst_hi=dst_hi: e.activation(
                            out=dst_hi, in_=src3, func=AF.Copy),
                            waits=[("t", t_pt)], inc="a")
                        v_pl = pl.op("vector", lambda e, src3=src3, dst_hi=dst_hi, dst_lo=dst_lo: e.tensor_tensor(
                            out=dst_lo, in0=src3, in1=dst_hi, op=ALU.subtract),
                            waits=[("a", a_ph)], inc="v")
                        pt_ev[ptbank] = [("a", a_ph), ("v", v_pl)]
                mm_bank_wait(4)
                pl.wait("tensor", ("v", pl.cnt["v"]), ("a", pl.cnt["a"]))
                t_pv = None
                for kc in range(KC):
                    vh = ar.bf16("Vth")[:, kc * 128:(kc + 1) * 128]
                    vl = ar.bf16("Vtl")[:, kc * 128:(kc + 1) * 128]
                    ph = ar.bf16("PT")[:, kc * 512:kc * 512 + 256]
                    plo = ar.bf16("PT")[:, kc * 512 + 256:kc * 512 + 512]
                    for pi, (lt, rt) in enumerate(((vh, ph), (vl, ph), (vh, plo))):
                        last = (kc == KC - 1 and pi == 2)
                        t_pv = pl.op("tensor", lambda e, lt=lt, rt=rt, kc=kc, pi=pi, last=last: e.matmul(
                            PS[4][:, :256], lt, rt, start=(kc == 0 and pi == 0), stop=last),
                            inc="t" if last else None)
                a_oh = pl.op("scalar", lambda e: e.activation(
                    out=ar.bf16("Oh"), in_=PS[4][:, :256], func=AF.Copy),
                    waits=[("t", t_pv)], inc="a")
                v_ol = pl.op("vector", lambda e: e.tensor_tensor(
                    out=ar.bf16("Ol"), in0=PS[4][:, :256], in1=ar.bf16("Oh"), op=ALU.subtract),
                    waits=[("a", a_oh)], inc="v")
                bank_free[4] = [("v", v_ol)]
                pl.wait("tensor", ("a", a_oh), ("v", v_ol))
                opj_free = {5: [], 0: []}
                last_b0 = None
                for s_ in (0, 1):
                    ev_list = []
                    for eg in range(4):
                        ob = 5 if eg % 2 == 0 else 0
                        pl.wait("tensor", *opj_free[ob])
                        opj_free[ob] = []
                        oh = ar.bf16("Oh")[:, s_ * 128:(s_ + 1) * 128]
                        ol = ar.bf16("Ol")[:, s_ * 128:(s_ + 1) * 128]
                        wh_ = ar.bf16("WOh")[:, eg * 512:(eg + 1) * 512]
                        wl_ = ar.bf16("WOl")[:, eg * 512:(eg + 1) * 512]
                        t_op = None
                        for pi, (lt, rt) in enumerate(((oh, wh_), (ol, wh_), (oh, wl_))):
                            t_op = pl.op("tensor", lambda e, lt=lt, rt=rt, pi=pi, ob=ob: e.matmul(
                                PS[ob][:, :], lt, rt, start=(pi == 0), stop=(pi == 2)),
                                inc="t" if pi == 2 else None)
                        dst = ar.f32("rstg")[:, eg * 512:(eg + 1) * 512]
                        ewaits = [("t", t_op)] + ([("sg", rstg_free)] if eg == 0 else [])
                        ev = pl.op("scalar", lambda e, dst=dst, s_=s_, ob=ob: e.activation(
                            out=dst, in_=PS[ob][:, :], func=AF.Copy, scale=tiny(T_RRC + s_)),
                            waits=ewaits, inc="a")
                        ev_list.append(("a", ev))
                        opj_free[ob] = [ev_list[-1]]
                        if ob == 0:
                            last_b0 = ("a", ev)
                        else:
                            bank_free[5] = [ev_list[-1]]
                    rdst = rs_in[q0 + s_ * 128:q0 + (s_ + 1) * 128, :]
                    rstg_free = pl.dma("gpsimd", rdst, ar.f32("rstg"), waits=ev_list,
                                       accum=(hidx > 0))
                if last_b0 is not None:
                    sc_ev[0] = sc_ev[0] + [last_b0]

        for hidx in range(A_HEADS):
            phase_a1(hidx)
            pl.barrier()
            if hidx == 0 and "vpre" in dbg:
                vp = dbg["vpre"].rearrange("(n p) s -> n p s", p=128)
                pl.dma("gpsimd", vp[0], ar.bf16("Vth"))
                pl.dma("gpsimd", vp[1], ar.bf16("Vtl"))
                pl.dma("gpsimd", dbg["vtf"], ar.f32("vtf"))
                if "w32d" in dbg:
                    pl.dma("gpsimd", dbg["w32d"], ar.f32("wq32")[:, 2 * 2048:3 * 2048])
                    pl.dma("gpsimd", dbg["whd"], ar.bf16("wqh")[:, 2 * 2048:3 * 2048])
                if "ps5" in dbg:
                    a_d = pl.op("scalar", lambda e: e.activation(
                        out=ar.f32("rstg")[:, 0:512], in_=PS[3][:, :], func=AF.Copy),
                        inc="a")
                    pl.dma("sync", dbg["ps5"], ar.f32("rstg")[:, 0:512],
                           waits=[("a", a_d)])
                pl.barrier()
            phase_a2(hidx, hidx)
            pl.barrier()

        if "rsin" in dbg:
            pl.dma("gpsimd", dbg["rsin"], rs_in)
            pl.barrier()
        if "ssb" in dbg:
            pl.dma("gpsimd", dbg["ssb"], ar.f32("S_sb"))
            pl.dma("gpsimd", dbg["psb"], ar.f32("P_sb"))
            pl.dma("gpsimd", dbg["pt"], ar.bf16("PT"))
            oh2 = dbg["oh"].rearrange("p (n c) -> n p c", c=256)
            pl.dma("gpsimd", oh2[0], ar.bf16("Oh"))
            pl.dma("gpsimd", oh2[1], ar.bf16("Ol"))
            pl.dma("gpsimd", dbg["rstg"], ar.f32("rstg"))
            pl.dma("gpsimd", dbg["tiny"], ar.f32("tiny"))
            pl.barrier()
        if "kvq" in dbg:
            kv = dbg["kvq"].rearrange("(n p) s -> n p s", p=128)
            for j, nm in enumerate(("KTh", "KTl", "QTh", "QTl", "Vth", "Vtl")):
                pl.dma("gpsimd", kv[j], ar.bf16(nm))
            pl.barrier()

        pl.wait("gpsimd", ("sg", pl.cnt["sg"]))
        pl.op("gpsimd", lambda e: e.collective_compute(
            "ReduceScatter", ALU.add,
            replica_groups=[[0, 1, 2, 3], [4, 5, 6, 7]],
            ins=[rs_in.opt()], outs=[rs_out.opt()]), inc="cc", real=False)

    if "F" in PHASES:
        ar.reset(g_base)
        ar.alloc("xoh", 8192)
        ar.alloc("xol", 8192)
        ar.alloc("hT", 256)
        ar.alloc("gusl", 4096)
        ar.alloc("wps", 2048)
        ar.alloc("gusl1", 4096)
        ar.alloc("wps1", 2048)
        ar.alloc("hbuf", 256)
        ar.alloc("hsil", 512)
        ar.alloc("sqs", 256)
        ar.alloc("pacc", 16384)
        ar.alias("xbufF0", "pacc", 0, D)
        ar.alias("xbufF1", "pacc", D, D)
        ar.alias("xnfF", "pacc", 2 * D, D)
        ar.alias("sqscrF", "pacc", 3 * D, D)
        ar.alias("ftrh", "pacc", 4 * D, 1024)
        ar.alias("ftrl", "pacc", 4 * D + 1024, 1024)

        xoh4 = ar.bf16("xoh").rearrange("p (dc t) -> p dc t", t=1024)
        xol4 = ar.bf16("xol").rearrange("p (dc t) -> p dc t", t=1024)

        def f_writer(i, a_hi, v_lo):
            src_h = ar.bf16("ftrh").rearrange("p (dc t) -> p dc t", t=128)
            pl.op("vector", lambda e, i=i, src_h=src_h: e.tensor_copy(
                out=xoh4[:, :, i * 128:(i + 1) * 128], in_=src_h),
                waits=[("a", a_hi)])
            src_l = ar.bf16("ftrl").rearrange("p (dc t) -> p dc t", t=128)
            v_c = pl.op("vector", lambda e, i=i, src_l=src_l: e.tensor_copy(
                out=xol4[:, :, i * 128:(i + 1) * 128], in_=src_l), inc="v")
            return ("v", v_c)

        emit_ln_tr_phase(x_own, 8, f_writer, "xnfF", "sqscrF",
                         "xbufF0", "xbufF1", ar.bf16("ftrh"), ar.bf16("ftrl"))
        pl.barrier()
        pl.wait("sync", ("cc", 1 if "Q" in PHASES else 0))

        for sb_ in range(8):
            pl.op("vector", lambda e, sb_=sb_: e.memset(
                ar.f32("pacc")[:, sb_ * 2048:(sb_ + 1) * 2048], 0.0))
            last = (sb_ == 7)
            pl.op("vector", lambda e, sb_=sb_: e.memset(tiny(T_SSQ + sb_), 0.0),
                  inc="v" if last else None, flush=True)
        v_zero = pl.cnt["v"]

        gusl4P = [ar.bf16("gusl").rearrange("p (dc w) -> p dc w", w=512),
                  ar.bf16("gusl1").rearrange("p (dc w) -> p dc w", w=512)]
        wps4P = [ar.bf16("wps").rearrange("p (hc e) -> p hc e", e=2048),
                 ar.bf16("wps1").rearrange("p (hc e) -> p hc e", e=2048)]
        agg = ag_out.rearrange("(rb r) c -> rb r c", rb=NCORE)
        gu_bank_free = {0: [], 1: []}
        slab_freeP = {0: 0, 1: 0}
        sd_prevF = 0
        pj_free = {3: [], 4: []}
        hT_free = 0
        hT4 = ar.bf16("hT").rearrange("p (hc hl c) -> p hc hl c", hc=2, hl=2)
        for hg in range(32):
            par = hg % 2
            gusl4 = gusl4P[par]
            wps4 = wps4P[par]
            for j, hcol in enumerate((256 * hg, HF + 256 * hg)):
                rb, cl = hcol // 2048, hcol % 2048
                src = agg[rb, 0:2048, cl:cl + 256].rearrange("(dc p) w -> p dc w", p=128)
                pl.dma("sync", gusl4[:, :, j * 256:(j + 1) * 256], src,
                       waits=[("t", slab_freeP[par]), ("sd", sd_prevF)] if j == 0 else ())
            for rb in range(NCORE):
                pTr = agg[rb, 2048:AGR, :].rearrange("r c -> (r c)").rearrange("(h e) -> h e", e=256)
                src = pTr[256 * hg:256 * (hg + 1), :].rearrange("(hc p) e -> p hc e", p=128)
                pl.dma("sync", wps4[:, :, rb * 256:(rb + 1) * 256], src)
            sd_sl = pl.cnt["sd"]
            sd_prevF = sd_sl
            for sub in range(8):
                gb = sub % 2
                pl.wait("tensor", *gu_bank_free[gb])
                gu_bank_free[gb] = []
                pl.wait("tensor", ("sd", sd_sl))
                t_gu = None
                first = True
                for dc in range(16):
                    for xi, xs in ((0, xoh4), (1, xol4)):
                        lt = xs[:, dc, sub * 128:(sub + 1) * 128]
                        rt = gusl4[:, dc, :]
                        last = (dc == 15) and (xi == 1)
                        t_gu = pl.op("tensor", lambda e, lt=lt, rt=rt, gb=gb, first=first, last=last: e.matmul(
                            PS[gb][:, :], lt, rt, start=first, stop=last),
                            inc="t" if last else None)
                        first = False
                pl.op("scalar", lambda e: e.preload_act_table(AF.Silu))
                a_sil = pl.op("scalar", lambda e, gb=gb: e.activation(
                    out=ar.f32("hsil"), in_=PS[gb][:, 0:512], func=AF.Silu),
                    waits=[("t", t_gu), ("v", pl.cnt["v"])], inc="a")
                v_h = pl.op("vector", lambda e, gb=gb: e.tensor_tensor(
                    out=ar.f32("hbuf"), in0=ar.f32("hsil")[:, 0:256], in1=PS[gb][:, 256:512], op=ALU.mult),
                    waits=[("a", a_sil)], inc="v")
                gu_bank_free[gb] = [("v", v_h)]
                pl.op("vector", lambda e: e.tensor_tensor(
                    out=ar.f32("sqs"), in0=ar.f32("hbuf"), in1=ar.f32("hbuf"), op=ALU.mult))
                rsl = T_SQC + sub % 2
                pl.op("vector", lambda e, rsl=rsl: e.tensor_reduce(
                    out=tiny(rsl), in_=ar.f32("sqs"), axis=AX.X, op=ALU.add), flush=True)
                pl.op("vector", lambda e, sub=sub, rsl=rsl: e.tensor_tensor(
                    out=tiny(T_SSQ + sub), in0=tiny(T_SSQ + sub), in1=tiny(rsl), op=ALU.add),
                    waits=[("v", v_zero)], flush=True)
                mm_bank_wait(2)
                pl.wait("tensor", ("v", pl.cnt["v"]), ("t", hT_free))
                t_ht = None
                for j in range(2):
                    t_ht = pl.op("tensor", lambda e, j=j: e.transpose(
                        PS[2][:, j * 128:(j + 1) * 128], ar.f32("hbuf")[:, j * 128:(j + 1) * 128], ident),
                        inc="t" if j == 1 else None)
                src2 = PS[2][:, 0:256].rearrange("p (hc c) -> p hc c", c=128)
                a_hh = pl.op("scalar", lambda e, src2=src2: e.activation(
                    out=hT4[:, :, 0, :], in_=src2, func=AF.Copy),
                    waits=[("t", t_ht)], inc="a")
                v_hl = pl.op("vector", lambda e, src2=src2: e.tensor_tensor(
                    out=hT4[:, :, 1, :], in0=src2, in1=hT4[:, :, 0, :], op=ALU.subtract),
                    waits=[("a", a_hh)], inc="v")
                bank_free[2] = [("v", v_hl)]
                for eg in range(4):
                    pjb = 3 + eg % 2
                    pl.wait("tensor", *pj_free[pjb])
                    pj_free[pjb] = []
                    pl.wait("tensor", ("a", a_hh), ("v", v_hl))
                    t_pj = None
                    first = True
                    for hc in range(2):
                        for hl in range(2):
                            lt = hT4[:, hc, hl, :]
                            rt = wps4[:, hc, eg * 512:(eg + 1) * 512]
                            last = (hc == 1) and (hl == 1)
                            t_pj = pl.op("tensor", lambda e, lt=lt, rt=rt, pjb=pjb, first=first, last=last: e.matmul(
                                PS[pjb][:, :], lt, rt, start=first, stop=last),
                                inc="t" if last else None)
                            first = False
                    dst = ar.f32("pacc")[:, sub * 2048 + eg * 512: sub * 2048 + (eg + 1) * 512]
                    v_pa = pl.op("vector", lambda e, pjb=pjb, dst=dst: e.tensor_tensor(
                        out=dst, in0=dst, in1=PS[pjb][:, :], op=ALU.add),
                        waits=[("t", t_pj)], inc="v")
                    pj_free[pjb] = [("v", v_pa)]
                hT_free = pl.cnt["t"]
            slab_freeP[par] = pl.cnt["t"]
        pl.barrier()

        G_OFF = ar.named["xoh"][0]
        gbuf = ar.h[:, G_OFF:G_OFF + 16384]
        a_g = 0
        for sub in range(8):
            pl.op("vector", lambda e, sub=sub: e.tensor_scalar(
                out=tiny(T_VAR), in0=tiny(T_SSQ + sub), scalar1=1.0 / HF,
                scalar2=RMS_EPS, op0=ALU.mult, op1=ALU.add),
                waits=[("a", pl.cnt["a"])], flush=True)
            v_rs = emit_rsqrt(T_VAR, T_FIN + sub, T_TMP)
            pl.wait("scalar", ("v", v_rs))
            pl.op("scalar", lambda e: e.preload_act_table(AF.Silu))
            a_g = pl.op("scalar", lambda e, sub=sub: e.activation(
                out=gbuf[:, sub * 2048:(sub + 1) * 2048],
                in_=ar.f32("pacc")[:, sub * 2048:(sub + 1) * 2048],
                func=AF.Silu, scale=tiny(T_FIN + sub)), inc="a")
        if "gbuf" in dbg:
            for sub in range(8):
                pl.dma("sync", dbg["gbuf"][sub * 128:(sub + 1) * 128, :],
                       gbuf[:, sub * 2048:(sub + 1) * 2048], waits=[("a", a_g)])
            pl.barrier()

    if "C" in PHASES:
        X_OFF = ar.named["pacc"][0] if "F" in PHASES else g_base
        xbC = ar.h[:, X_OFF:X_OFF + 2048]
        sbC = ar.h[:, X_OFF + 2048:X_OFF + 4096]
        G_OFF2 = ar.named["xoh"][0] if "F" in PHASES else g_base + 4096
        gbuf2 = ar.h[:, G_OFF2:G_OFF2 + 16384]
        pl.barrier()
        if "A" in PHASES:
            pl.wait("sync", ("cc", 2 if "Q" in PHASES else 1))
        if "rsout" in dbg:
            pl.dma("sync", dbg["rsout"], rs_out)
        so_fin = 0
        for sub in range(8):
            vdone = ([("v", pl.cnt["v"]), ("sd", so_fin)] if sub >= 1 else [])
            sd_xc = pl.dma("sync", xbC, x_own[sub * 128:(sub + 1) * 128, :], waits=vdone)
            sd_sc = pl.dma("sync", sbC, rs_out[sub * 128:(sub + 1) * 128, :])
            g_sl = gbuf2[:, sub * 2048:(sub + 1) * 2048]
            pl.op("vector", lambda e, g_sl=g_sl: e.tensor_tensor(
                out=sbC, in0=sbC, in1=g_sl, op=ALU.mult),
                waits=[("sd", sd_sc), ("a", pl.cnt["a"])])
            v_o = pl.op("vector", lambda e: e.tensor_tensor(
                out=sbC, in0=sbC, in1=xbC, op=ALU.add),
                waits=[("sd", sd_xc)], inc="v")
            so_fin = pl.dma("sync", out_c[sub * 128:(sub + 1) * 128, :], sbC,
                            waits=[("v", v_o)])
        pl.wait("sync", ("sd", so_fin))
        pl.wait("gpsimd", ("sg", pl.cnt["sg"]))

    pl.check()

    import contextlib
    with contextlib.ExitStack() as stack:
        SM = {}
        for s_ in ("sd", "sg", "sv", "sa", "t", "v", "a", "g", "cc", "bar"):
            SM[s_] = stack.enter_context(nc.semaphore(s_))
        block = stack.enter_context(nc.Block())
        for eng in ENGS:
            def body(e, eng=eng):
                for f in pl.ops[eng]:
                    f(e, SM)
            getattr(block, eng)(body)
    return nc


def _host_inputs(L0_attn_x, L0_attn_W_qkv, L0_attn_W_out, L0_ffn_w_gate_up, L0_ffn_w_proj):
    x = np.ascontiguousarray(np.asarray(L0_attn_x, dtype=np.float32))
    wqkv = np.asarray(L0_attn_W_qkv, dtype=np.float32)
    wout = np.asarray(L0_attn_W_out, dtype=np.float32)
    wgu = np.ascontiguousarray(np.asarray(L0_ffn_w_gate_up, dtype=np.float32))
    wp = np.ascontiguousarray(np.asarray(L0_ffn_w_proj, dtype=np.float32))
    wqkvT = np.ascontiguousarray(wqkv.T)
    woutT = np.ascontiguousarray(wout.T)
    ident = np.eye(128, dtype=np.float32)
    masks = np.zeros((128, 4 * 512), dtype=np.float32)
    for j in range(4):
        p = np.arange(128)[:, None]
        f = np.arange(512)[None, :]
        masks[:, j * 512:(j + 1) * 512] = np.where(f <= 128 * j + p, 0.0, -1.0e5)
    masks = masks.astype(ml_dtypes.bfloat16)
    in_maps = []
    for c in range(NCORE):
        seq, l = c // 4, c % 4
        h0 = 4 * l
        qcols = np.arange(h0 * 128, (h0 + 4) * 128)
        cols = np.concatenate([qcols, 2048 + qcols, 4096 + qcols])
        in_maps.append({
            "x_seq": x[seq],
            "x_own": np.ascontiguousarray(x[seq, l * OWN:(l + 1) * OWN]),
            "wqkvT": np.ascontiguousarray(wqkvT[:, cols]),
            "woutT": np.ascontiguousarray(woutT[h0 * 128:(h0 + 4) * 128, :]),
            "wgu_sh": np.ascontiguousarray(wgu[c * (GU // NCORE):(c + 1) * (GU // NCORE)]),
            "wp_sh": np.ascontiguousarray(wp[c * (D // NCORE):(c + 1) * (D // NCORE)]),
            "ident_in": ident,
            "masks_in": masks,
        })
    return in_maps


def kernel(L0_attn_x, L0_attn_W_qkv, L0_attn_W_out, L0_ffn_w_gate_up, L0_ffn_w_proj):
    in_maps = _host_inputs(L0_attn_x, L0_attn_W_qkv, L0_attn_W_out,
                           L0_ffn_w_gate_up, L0_ffn_w_proj)
    nc = build_core_program()
    res = run_bass_kernel_spmd(nc, in_maps, core_ids=list(range(NCORE)))
    out = np.empty((2, S, D), dtype=np.float32)
    for c in range(NCORE):
        seq, l = c // 4, c % 4
        out[seq, l * OWN:(l + 1) * OWN] = np.asarray(res.results[c]["out_c"])
    return out
